# version 25
# baseline (speedup 1.0000x reference)
import numpy as np
import concourse.bass as bass
import concourse.mybir as mybir
from concourse.bass_utils import run_bass_kernel_spmd

F32 = mybir.dt.float32
BF16 = mybir.dt.bfloat16
AF = mybir.ActivationFunctionType
ALU = mybir.AluOpType
AX = mybir.AxisListType

S = 2048
D = 1024
H = 16
NE = 8
NT = S // 128
NDC = D // 128
PAIR = 193

ENGS = ("sync", "gpsimd", "tensor", "vector", "scalar")
SEM_OWNER = {"sT": "tensor", "sV": "vector", "sS": "scalar", "sG": "gpsimd"}


class Em:
    def __init__(self):
        self.ops = {k: [] for k in ENGS}
        self.cnt = {}
        self.lastw = {k: {} for k in ENGS}

    def emit(self, eng, fn, waits=(), inc=None):
        w = []
        for t in waits:
            if t is None:
                continue
            s, v = t
            if SEM_OWNER.get(s) == eng:
                continue
            if self.lastw[eng].get(s, -1) < v:
                w.append((s, v))
                self.lastw[eng][s] = v
        tok = None
        if inc is not None:
            s, a = inc
            self.cnt[s] = self.cnt.get(s, 0) + a
            tok = (s, self.cnt[s])
        self.ops[eng].append([w, fn, inc])
        return tok

    def bump(self, eng, sem):
        op = self.ops[eng][-1]
        assert op[2] is None
        self.cnt[sem] = self.cnt.get(sem, 0) + 1
        op[2] = (sem, self.cnt[sem])
        return (sem, self.cnt[sem])

    def snap(self, sem):
        return (sem, self.cnt.get(sem, 0))

    def dma(self, eng, out_ap, in_ap, waits=(), sem=None, **kw):
        if sem is None:
            sem = "dG" if eng == "gpsimd" else "dS"
        return self.emit(eng, lambda e, o=out_ap, i=in_ap, kw=kw: e.dma_start(o, i, **kw),
                         waits=waits, inc=(sem, 16))


class Arena:
    def __init__(self, nbytes):
        self.free = [(0, nbytes)]
        self.allocs = {}

    def alloc(self, name, nbytes, align=64):
        for i, (off, sz) in enumerate(self.free):
            aoff = (off + align - 1) // align * align
            if aoff + nbytes <= off + sz:
                rep = [p for p in ((off, aoff - off),
                                   (aoff + nbytes, off + sz - (aoff + nbytes))) if p[1] > 0]
                self.free[i:i + 1] = rep
                self.allocs[name] = (aoff, nbytes)
                return aoff
        raise MemoryError(f"arena OOM for {name} ({nbytes}B); free={self.free}")

    def freep(self, name):
        off, sz = self.allocs.pop(name)
        self.free.append((off, sz))
        self.free.sort()
        m = []
        for o2, s2 in self.free:
            if m and m[-1][0] + m[-1][1] == o2:
                m[-1] = (m[-1][0], m[-1][1] + s2)
            else:
                m.append((o2, s2))
        self.free = m


def build_nc(debug=False):
    nc = bass.Bass("TRN2", target_bir_lowering=False, detect_race_conditions=False)

    x_d = nc.dram_tensor("x", [S, D], F32, kind="ExternalInput").ap()
    wq_d = nc.dram_tensor("wq", [D, D], F32, kind="ExternalInput").ap()
    wk_d = nc.dram_tensor("wk", [D, D], F32, kind="ExternalInput").ap()
    wv_d = nc.dram_tensor("wv", [D, D], F32, kind="ExternalInput").ap()
    wo_d = nc.dram_tensor("wo", [D, D], F32, kind="ExternalInput").ap()
    wr_d = nc.dram_tensor("wr", [D, NE], F32, kind="ExternalInput").ap()
    wi_d = nc.dram_tensor("wi", [NE, D, D], F32, kind="ExternalInput").ap()
    wm_d = nc.dram_tensor("wm", [NE, D, D], F32, kind="ExternalInput").ap()
    cos_d = nc.dram_tensor("costab", [128, S], F32, kind="ExternalInput").ap()
    sin_d = nc.dram_tensor("sintab", [128, S], F32, kind="ExternalInput").ap()
    id_d = nc.dram_tensor("identb", [128, 128], BF16, kind="ExternalInput").ap()
    tm_d = nc.dram_tensor("trimask", [128, 128], BF16, kind="ExternalInput").ap()
    sel_d = nc.dram_tensor("selmat", [16, 1024], BF16, kind="ExternalInput").ap()
    out_d = nc.dram_tensor("out", [S, D], F32, kind="ExternalOutput").ap()
    x2_spill = nc.dram_tensor("x2spill", [S, D], F32, kind="Internal").ap()
    dbg = {}
    if debug:
        for nm, shape, dt in [("dbg_qh", [D, S], BF16), ("dbg_kh", [D, S], BF16),
                              ("dbg_vaug", [NT * 128, PAIR * 8], BF16),
                              ("dbg_attnoutT", [D, S], BF16),
                              ("dbg_xmid", [S, D], F32), ("dbg_x2", [S, D], F32),
                              ("dbg_w", [S, NE], F32), ("dbg_oacc", [S, D], F32),
                              ("dbg_lg", [S, NE], F32), ("dbg_x2hiT", [D, S], BF16),
                              ("dbg_x2loT", [D, S], BF16), ("dbg_r8", [S, 16], F32)]:
            dbg[nm] = nc.dram_tensor(nm, shape, dt, kind="ExternalOutput").ap()

    em = Em()
    AR_BYTES = 203 * 1024 - 1024
    ar = Arena(AR_BYTES)
    ctx = []

    def enter(g):
        t = g.__enter__()
        ctx.append(g)
        return t

    arena_bf = enter(nc.sbuf_tensor("arena", [128, AR_BYTES // 2], BF16))
    arena_f32h = arena_bf.bitcast(F32)
    ident = enter(nc.sbuf_tensor("id_sb", [128, 128], BF16))
    ident_f = enter(nc.sbuf_tensor("idf_sb", [128, 128], F32))
    trimask = enter(nc.sbuf_tensor("tm_sb", [128, 128], BF16))
    consts = enter(nc.sbuf_tensor("consts", [128, 16], F32))
    selmat = enter(nc.sbuf_tensor("selmat_sb", [16, 1024], BF16))
    w_sb = enter(nc.sbuf_tensor("w_sb", [128, NT * NE], F32))
    wr_f = enter(nc.sbuf_tensor("wr_f", [128, NDC * NE], F32))
    wr_hi = enter(nc.sbuf_tensor("wr_hi", [128, NDC * NE], BF16))
    wr_lo = enter(nc.sbuf_tensor("wr_lo", [128, NDC * NE], BF16))
    rsb = enter(nc.sbuf_tensor("rsb", [128, 16], F32))
    rrow = enter(nc.sbuf_tensor("rrow", [16, 256], BF16))
    lnsc = [enter(nc.sbuf_tensor(f"lnsc{i}", [128, 16], F32)) for i in range(3)]
    r8 = enter(nc.sbuf_tensor("r8", [128, 48], F32))
    lgdump = enter(nc.sbuf_tensor("lgdump", [128, 64], F32)) if debug else None
    lgd_user = [None, None, None, None]

    pS = [enter(nc.psum_tensor(f"pS{i}", [128, 1024], F32)) for i in range(2)]
    pAV = [enter(nc.psum_tensor(f"pAV{i}", [128, 512], F32)) for i in range(2)]
    pZR = enter(nc.psum_tensor("pZR", [128, 512], F32))
    pB = enter(nc.psum_tensor("pB", [128, 512], F32))
    pS_bf = [p.bitcast(BF16) for p in pS]
    pAV_bf = [p.bitcast(BF16) for p in pAV]

    sems = {s: enter(nc.semaphore(s)) for s in ("dG", "dGC", "dS", "dC", "dR", "dW0", "dW1", "dX", "dSP", "dB", "dO", "sT", "sV", "sS")}

    CONST_VALS = {0: 0.125, 1: -20.0, 2: 1e-5, 3: -0.5, 4: 0.0, 5: 1.0,
                  6: 1.0 / D, 7: -1.0}
    CST = {v: consts[:, i:i + 1] for i, v in CONST_VALS.items()}

    ARW = AR_BYTES // 2

    def bf(off, cols):
        assert off % 2 == 0
        return arena_bf[:, off // 2: off // 2 + cols]

    def f32(off, cols):
        assert off % 4 == 0
        return arena_f32h[:, off // 4: off // 4 + cols]

    def bf_ap(off_bytes, dims):
        return bass.AP(arena_bf, off_bytes // 2, [[ARW, 128]] + dims)

    for i, v in CONST_VALS.items():
        em.emit("vector", lambda e, i=i, v=v: e.memset(consts[:, i:i + 1], v))
    em.emit("vector", lambda e: e.memset(r8[:, 8:48], 0.0))
    t_cst = em.bump("vector", "sV")
    t_ones = em.dma("sync", selmat[:, :], sel_d[:, :], sem="dC")
    t_id = em.dma("sync", ident[:, :], id_d[:, :], sem="dC")
    t_idf = em.dma("gpsimd", ident_f[:, :], id_d[:, :], sem="dGC")
    t_tm = em.dma("sync", trimask[:, :], tm_d[:, :], sem="dC")

    o_cos = ar.alloc("cos", S * 4)
    o_sin = ar.alloc("sin", S * 4)
    cosT = f32(o_cos, S)
    sinT = f32(o_sin, S)
    t_cos = em.dma("sync", cosT, cos_d[:, :], sem="dC")
    t_sin = em.dma("sync", sinT, sin_d[:, :], sem="dC")

    t_wrf = em.dma("sync", wr_f[:, :],
                   bass.AP(wr_d.tensor, 0, [[NE, 128], [NE * 128, NDC], [1, NE]]), sem="dC")
    t_const = ("dC", 96)
    t_wrhi = em.emit("vector", lambda e: e.tensor_copy(wr_hi[:, :], wr_f[:, :]),
                     waits=[t_const], inc=("sV", 1))
    em.emit("vector", lambda e: e.tensor_copy(consts[:, 8:9], consts[:, 5:6]))
    em.emit("vector", lambda e: e.tensor_copy(consts[:, 9:10], consts[:, 5:6]))
    t_wrlo = em.emit("vector",
                     lambda e: e.tensor_tensor(wr_lo[:, :], wr_f[:, :], wr_hi[:, :], ALU.subtract),
                     waits=[t_wrhi], inc=("sV", 1))

    o_xb = ar.alloc("xb", NT * 1024 * 2)
    xb = [bf(o_xb + tt * 2048, 1024) for tt in range(NT)]
    t_xp = [em.dma("gpsimd", bf_ap(o_xb + q4 * 4 * 2048, [[1024, 4], [1, 1024]]),
                   bass.AP(x_d.tensor, q4 * 4 * 128 * D, [[D, 128], [128 * D, 4], [1, D]]))
            for q4 in range(4)]
    t_x = t_xp[-1]

    o_xT = ar.alloc("xT", NDC * S * 2)
    xT = [bf(o_xT + dc * S * 2, S) for dc in range(NDC)]

    pS_ready = [t_x, t_x]
    tok_xT_last = None
    for dc in range(NDC):
        g = dc % 2
        for tt in range(NT):
            half, sl = tt // 8, tt % 8

            def mk(t, dc=dc, tt=tt, g=g, half=half, sl=sl):
                return t.transpose(pS_bf[g][:, half * 1024 + sl * 128: half * 1024 + (sl + 1) * 128],
                                   xb[tt][:, dc * 128:(dc + 1) * 128], ident[:, :])
            w = [t_xp[min(tt // 4, 3)], t_const, pS_ready[g]] if tt % 4 == 0 else []
            em.emit("tensor", mk, waits=w)
        mmtok = em.bump("tensor", "sT")
        tok_xT_last = em.emit("vector",
                              lambda e, dc=dc, g=g: e.tensor_copy(xT[dc][:, :], pS_bf[g][:, 0:2048]),
                              waits=[mmtok], inc=("sV", 1))
        pS_ready[g] = tok_xT_last
    ar.freep("xb")

    o_qh = ar.alloc("qh", NDC * S * 2)
    o_kh = ar.alloc("kh", NDC * S * 2)
    qh = [bf(o_qh + i * S * 2, S) for i in range(NDC)]
    kh = [bf(o_kh + i * S * 2, S) for i in range(NDC)]
    o_vaug_t = [ar.alloc(f"vaug{tt}", PAIR * 8 * 2, align=128) for tt in range(NT)]
    vaug = [bf(o_vaug_t[tt], PAIR * 8) for tt in range(NT)]

    o_wstg = ar.alloc("wstg", 2 * 1024 * 2)
    wstg = [bf(o_wstg + s * 2048, 1024) for s in range(2)]
    stg_user = [t_x, t_x]

    def load_transpose_weight(wd, o_dstT, permuted, extra=()):
        last = None
        for i in range(NDC):
            s = i % 2
            ld = None
            if permuted:
                c, par = i // 2, i % 2
                for j in range(4):
                    hh = 4 * c + j
                    base = 64 * hh + par
                    src = bass.AP(wd.tensor, base * D, [[2 * D, 32], [1, D]])
                    ld = em.dma("gpsimd", wstg[s][32 * j:32 * j + 32, :], src, sem=f"dW{s}",
                                waits=([stg_user[s]] + list(extra)) if j == 0 else [])
            else:
                src = bass.AP(wd.tensor, i * 128 * D, [[D, 128], [1, D]])
                ld = em.dma("gpsimd", wstg[s][:, :], src, sem=f"dW{s}",
                            waits=[stg_user[s]] + list(extra))
            g = i % 2
            for dc in range(NDC):
                def mk(t, dc=dc, s=s, g=g):
                    return t.transpose(pS_bf[g][:, dc * 128:(dc + 1) * 128],
                                       wstg[s][:, dc * 128:(dc + 1) * 128], ident[:, :])
                w = [pS_ready[g], ld, t_const] if dc == 0 else []
                em.emit("tensor", mk, waits=w)
            mmtok = em.bump("tensor", "sT")
            stg_user[s] = mmtok

            def ev(e, i=i, g=g, o_dstT=o_dstT):
                dst = bf_ap(o_dstT + i * 256, [[1024, NDC], [1, 128]])
                return e.tensor_copy(dst, pS_bf[g][:, 0:1024])
            last = em.emit("vector", ev, waits=[mmtok], inc=("sV", 1))
            pS_ready[g] = last
        return last

    o_rope = ar.alloc("rope", 2 * S * 2)
    ropeEO = [[bf(o_rope + p * S * 2, S) for p in range(2)]]
    o_rtmp = ar.alloc("rtmp", 2 * 512 * 4)
    rtmp = [f32(o_rtmp + i * 2048, 512) for i in range(2)]

    proj_ps = [pAV[0], pAV[1], pZR, pB]
    proj_reader = [tok_xT_last] * 4
    rope_war = [None]

    def proj_qk(wT, wt_last, dst_heads):
        repack_last = None
        for c in range(4):
            rope_done = None
            for tc in range(4):
                rot = tc % 2
                psE, psO = proj_ps[2 * rot], proj_ps[2 * rot + 1]
                for dc in range(NDC):
                    def mkE(t, c=c, tc=tc, dc=dc, psE=psE, wT=wT):
                        return t.matmul(psE[:, 0:512],
                                        wT[dc][:, (2 * c) * 128:(2 * c + 1) * 128],
                                        xT[dc][:, tc * 512:(tc + 1) * 512],
                                        start=(dc == 0), stop=(dc == NDC - 1))
                    w = [proj_reader[2 * rot], wt_last] if dc == 0 else []
                    em.emit("tensor", mkE, waits=w)
                mmE = em.bump("tensor", "sT")
                for dc in range(NDC):
                    def mkO(t, c=c, tc=tc, dc=dc, psO=psO, wT=wT):
                        return t.matmul(psO[:, 0:512],
                                        wT[dc][:, (2 * c + 1) * 128:(2 * c + 2) * 128],
                                        xT[dc][:, tc * 512:(tc + 1) * 512],
                                        start=(dc == 0), stop=(dc == NDC - 1))
                    w = [proj_reader[2 * rot + 1]] if dc == 0 else []
                    em.emit("tensor", mkO, waits=w)
                mmO = em.bump("tensor", "sT")
                cs = cosT[:, tc * 512:(tc + 1) * 512]
                sn = sinT[:, tc * 512:(tc + 1) * 512]
                eE, eO = psE[:, 0:512], psO[:, 0:512]
                oE = ropeEO[0][0][:, tc * 512:(tc + 1) * 512]
                oO = ropeEO[0][1][:, tc * 512:(tc + 1) * 512]
                w0 = [mmE, mmO, t_const]
                if rope_war[0] is not None and tc == 0:
                    w0 = w0 + [rope_war[0]]
                em.emit("vector", lambda e, eE=eE, cs=cs: e.tensor_tensor(rtmp[0], eE, cs, ALU.mult),
                        waits=w0)
                em.emit("vector", lambda e, eO=eO, sn=sn: e.tensor_tensor(rtmp[1], eO, sn, ALU.mult))
                em.emit("vector", lambda e, oE=oE: e.tensor_tensor(oE, rtmp[0], rtmp[1], ALU.subtract))
                em.emit("vector", lambda e, eE=eE, sn=sn: e.tensor_tensor(rtmp[0], eE, sn, ALU.mult))
                em.emit("vector", lambda e, eO=eO, cs=cs: e.tensor_tensor(rtmp[1], eO, cs, ALU.mult))
                rope_done = em.emit("vector",
                                    lambda e, oO=oO: e.tensor_tensor(oO, rtmp[0], rtmp[1], ALU.add),
                                    inc=("sV", 1))
                proj_reader[2 * rot] = rope_done
                proj_reader[2 * rot + 1] = rope_done
            for j in range(4):
                hh = 4 * c + j
                i, pbase = hh // 2, (hh % 2) * 64
                em.dma("sync", dst_heads[i][pbase:pbase + 32, :],
                       ropeEO[0][0][32 * j:32 * j + 32, :], waits=[rope_done], sem="dR")
                repack_last = em.dma("sync", dst_heads[i][pbase + 32:pbase + 64, :],
                                     ropeEO[0][1][32 * j:32 * j + 32, :], sem="dR")
            rope_war[0] = repack_last
        return repack_last

    o_wT = ar.alloc("wT", NDC * 1024 * 2)
    wT_tiles = [bf(o_wT + i * 1024 * 2, 1024) for i in range(NDC)]
    tq_last = load_transpose_weight(wq_d, o_wT, True)
    q_ready = proj_qk(wT_tiles, tq_last, qh)
    t_qproj_T = em.snap("sT")
    tk_last = load_transpose_weight(wk_d, o_wT, True, extra=[t_qproj_T])
    k_ready = proj_qk(wT_tiles, tk_last, kh)
    t_kproj_T = em.snap("sT")
    ar.freep("rope")
    ar.freep("rtmp")
    ar.freep("cos")
    ar.freep("sin")
    vfill_last = None
    for tt in range(NT):
        em.emit("vector", lambda e, tt=tt: e.memset(
            bf_ap(o_vaug_t[tt] + 66 * 2, [[PAIR, 8], [1, 63]]), 0.0),
            waits=[k_ready] if tt == 0 else [])
        vfill_last = em.emit("vector", lambda e, tt=tt: e.memset(
            bf_ap(o_vaug_t[tt] + 64 * 2, [[PAIR, 8], [1, 2]]), 1.0),
            inc=("sV", 1))
    tv_last = load_transpose_weight(wv_d, o_wT, False, extra=[t_kproj_T])
    v_last = None
    for tt in range(NT):
        for half in range(2):
            rot = (tt * 2 + half) % 4
            ps = proj_ps[rot]
            for dc in range(NDC):
                def mkV(t, tt=tt, half=half, dc=dc, ps=ps):
                    return t.matmul(ps[:, 0:512], xT[dc][:, tt * 128:(tt + 1) * 128],
                                    wT_tiles[dc][:, half * 512:(half + 1) * 512],
                                    start=(dc == 0), stop=(dc == NDC - 1))
                w = [proj_reader[rot], tv_last] if dc == 0 else []
                em.emit("tensor", mkV, waits=w)
            mmv = em.bump("tensor", "sT")

            def evV(e, tt=tt, half=half, ps=ps):
                src = bass.AP(ps, 0, [[512, 128], [128, 4], [64, 2], [1, 64]])
                dst = bf_ap(o_vaug_t[tt] + half * 4 * PAIR * 2,
                            [[PAIR, 4], [129, 2], [1, 64]])
                return e.tensor_copy(dst, src)
            v_last = em.emit("vector", evV, waits=[mmv, vfill_last], inc=("sV", 1))
            proj_reader[rot] = v_last
    t_vproj_T = em.snap("sT")
    ar.freep("xT")
    ar.freep("wT")

    o_woT = ar.alloc("woT", NDC * 1024 * 2)
    woT = [bf(o_woT + i * 1024 * 2, 1024) for i in range(NDC)]

    o_attnU = ar.alloc("attnU", 8 * 512 * 2)
    attnU = [bf(o_attnU + sl2 * 1024, 512) for sl2 in range(8)]
    o_aoT = ar.alloc("attnoutT", NDC * S * 2)
    aoT = [bf(o_aoT + i * S * 2, S) for i in range(NDC)]
    o_zsb = ar.alloc("zsb", 2 * S * 4)
    Zsb = f32(o_zsb, 2 * S)

    attnU_user = [v_last] * 8
    pAV_user = [v_last, v_last]
    pZR_user = [v_last, v_last]
    pB_user = [v_last, v_last]
    dance_sV_last = [t_cst]
    slot_i = [0]
    pending = []
    wo_done = [None]
    last_av_tok = [None]

    def flush_dance(n):
        for _ in range(min(n, len(pending))):
            pending.pop(0)()

    def emit_dance(h, tok_evacs_done):
        zr = 64 if h % 2 == 0 else 0
        zoff = ((h // 2) % 2) * S
        i, pb = h // 2, (h % 2) * 64
        state = {}

        def stage0():
            for j in range(16):
                def mkz(t, j=j, zr=zr, zoff=zoff):
                    return t.transpose(pZR[:, j:j + 1],
                                       Zsb[zr:zr + 1, zoff + j * 128:zoff + (j + 1) * 128],
                                       consts[zr:zr + 1, 5:6])
                w = [tok_evacs_done, pZR_user[0]] if j == 0 else []
                em.emit("tensor", mkz, waits=w)
            tokz = em.bump("tensor", "sT")
            em.emit("vector", lambda e: e.reciprocal(rsb[:, 0:16], pZR[:, 0:16]), waits=[tokz])
            rc = em.bump("vector", "sV")
            em.emit("tensor", lambda t: t.transpose(pZR[0:16, 16:144], rsb[:, 0:16], ident_f[:, :]),
                    waits=[rc, t_idf])
            tokr = em.bump("tensor", "sT")
            em.emit("vector", lambda e: e.tensor_copy(rrow[:, 0:128], pZR[0:16, 16:144]), waits=[tokr])
            state["rt"] = em.bump("vector", "sV")
            pZR_user[0] = state["rt"]

        def d_qc(qc):
            for j2 in range(4):
                seg = 4 * qc + j2

                def mkb(t, seg=seg, j2=j2, pb=pb):
                    tp = (0, 64) if pb else None
                    return t.matmul(pB[pb:pb + 64, j2 * 128:(j2 + 1) * 128],
                                    selmat[0:16, 64 * seg:64 * seg + 64],
                                    rrow[0:16, 0:128], start=True, stop=True,
                                    tile_position=tp)
                w = [state["rt"], t_const, pB_user[0]] if j2 == 0 else []
                em.emit("tensor", mkb, waits=w)
            tokb = em.bump("tensor", "sT")

            def mknorm(e, qc=qc, i=i, pb=pb):
                sl2 = aoT[i][pb:pb + 64, qc * 512:(qc + 1) * 512]
                return e.tensor_tensor(sl2, sl2, pB[pb:pb + 64, 0:512], ALU.mult)
            em.emit("vector", mknorm, waits=[tokb])
            tokn = em.bump("vector", "sV")
            pB_user[0] = tokn
            dance_sV_last[0] = tokn

        pending.append(stage0)
        for qc in range(4):
            pending.append(lambda qc=qc: d_qc(qc))

    import os as _os2
    _HLIM = int(_os2.environ.get("DBG_ATT_H", H))
    gp_ctr = [0]
    for hp in range(max(1, _HLIM // 2)):
        i = hp
        heads = []
        for par in range(2):
            h = 2 * hp + par
            heads.append(dict(
                h=h, pb=par * 64,
                lhs_q=qh[i][par * 64:par * 64 + 64, :],
                lhs_k=kh[i][par * 64:par * 64 + 64, :],
                av_ps=pAV[par], vbase=hp * PAIR,
                exp_toks=None, pair_slots=None, tok_av=None))
        for qc in range(int(_os2.environ.get("DBG_ATT_QC", 4))):
            if hp == 0 and qc == 2:
                wo_done[0] = load_transpose_weight(wo_d, o_woT, False, extra=[t_vproj_T])
            flush_dance(3)
            K = 4 * qc + 4
            npairs = K // 2
            for hd in heads:
                hd["exp_toks"] = [None] * npairs
                hd["pair_slots"] = [None] * npairs

            def emit_spair2(p, qc=qc):
                gs = (gp_ctr[0] % 2, (gp_ctr[0] + 1) % 2)
                gp_ctr[0] += 2
                diag = 2 * p >= 4 * qc
                toks = [None, None]
                for hi in range(2):
                    hd, g = heads[hi], gs[hi]
                    lhs_k, lhs_q = hd["lhs_k"], hd["lhs_q"]
                    for sub in range(2):
                        kc = 2 * p + sub

                        def mkS(t, kc=kc, g=g, sub=sub, qc=qc, lhs_k=lhs_k,
                                lhs_q=lhs_q, diag=diag):
                            return t.matmul(pS[g][:, sub * 512:(sub + 1) * 512],
                                            lhs_k[:, kc * 128:(kc + 1) * 128],
                                            lhs_q[:, qc * 512:(qc + 1) * 512],
                                            start=True, stop=not diag)
                        w = [pS_ready[g]] if sub == 0 else []
                        if hp == 0 and qc == 0 and p == 0 and sub == 0 and hi == 0:
                            w = w + [q_ready, k_ready]
                        em.emit("tensor", mkS, waits=w)
                        if diag:
                            r = kc - 4 * qc

                            def mkM(t, g=g, sub=sub, r=r):
                                return t.matmul(pS[g][:, sub * 512 + r * 128: sub * 512 + (r + 1) * 128],
                                                ident[:, :], trimask[:, :], start=False, stop=True)
                            em.emit("tensor", mkM, waits=[t_const])
                    toks[hi] = em.bump("tensor", "sT")
                for hi in range(2):
                    hd, g = heads[hi], gs[hi]
                    sl2 = slot_i[0]
                    slot_i[0] = (sl2 + 2) % 8
                    hd["pair_slots"][p] = sl2

                    def mkexp(e, g=g, sl2=sl2):
                        return e.activation(bf(o_attnU + sl2 * 1024, 1024), pS[g][:, 0:1024],
                                            AF.Exp, bias=CST[-20.0], scale=CST[0.125])
                    hd["exp_toks"][p] = em.emit("scalar", mkexp,
                                                waits=[toks[hi], attnU_user[sl2],
                                                       attnU_user[sl2 + 1], t_cst, t_const],
                                                inc=("sS", 1))
                    pS_ready[g] = hd["exp_toks"][p]

            def emit_avpair(hd, p, qc=qc, K=K, extraw=None):
                tokav = None
                av_ps = hd["av_ps"]
                par = hd["pb"] // 64
                vbase = hd["vbase"]
                for sub in range(2):
                    kc = 2 * p + sub
                    sl2 = hd["pair_slots"][p] + sub
                    r = kc - 4 * qc
                    rstart = 128 * r if r > 0 else 0
                    if par == 0:
                        lhsT = vaug[kc][:, vbase:vbase + 65]
                        rows = (0, 65)
                    else:
                        lhsT = vaug[kc][:, vbase + 65:vbase + 193]
                        rows = (0, 128)

                    def mkav(t, kc=kc, sl2=sl2, rstart=rstart, lhsT=lhsT, rows=rows,
                             av_ps=av_ps, K=K):
                        return t.matmul(av_ps[rows[0]:rows[1], rstart:512], lhsT,
                                        attnU[sl2][:, rstart:512],
                                        start=(kc == 0), stop=(kc == K - 1))
                    w = [hd["exp_toks"][p], extraw]
                    if kc == 0:
                        w = w + [pAV_user[par], v_last]
                    em.emit("tensor", mkav, waits=w)
                tokav = em.bump("tensor", "sT")
                attnU_user[hd["pair_slots"][p]] = tokav
                attnU_user[hd["pair_slots"][p] + 1] = tokav
                hd["tok_av"] = tokav
                return tokav

            emit_spair2(0)
            for p in range(1, npairs):
                emit_spair2(p)
                emit_avpair(heads[0], p - 1)
                emit_avpair(heads[1], p - 1)
            emit_avpair(heads[0], npairs - 1)
            emit_avpair(heads[1], npairs - 1)
            last_av_tok[0] = heads[1]["tok_av"]

            for hd in heads:
                par = hd["pb"] // 64
                zr = 64 if par == 0 else 0
                orow = 0 if par == 0 else 64
                av_ps = hd["av_ps"]
                pb = hd["pb"]

                def ev_out(e, i=i, pb=pb, qc=qc, orow=orow, av_ps=av_ps):
                    return e.tensor_copy(aoT[i][pb:pb + 64, qc * 512:(qc + 1) * 512],
                                         av_ps[orow:orow + 64, 0:512])
                em.emit("vector", ev_out, waits=[hd["tok_av"]])

                zoff2 = (hp % 2) * S

                def ev_z(e, zr=zr, qc=qc, av_ps=av_ps, zoff2=zoff2):
                    return e.tensor_copy(Zsb[zr:zr + 1, zoff2 + qc * 512:zoff2 + (qc + 1) * 512],
                                         av_ps[zr:zr + 1, 0:512])
                em.emit("vector", ev_z)
                tok_ev = em.bump("vector", "sV")
                pAV_user[par] = tok_ev
                if qc == 3:
                    emit_dance(hd["h"], tok_ev)
    flush_dance(1000)
    t_attn_T = em.snap("sT")
    t_attn_V = dance_sV_last[0]

    if debug:
        for i2 in range(NDC):
            em.dma("sync", bass.AP(dbg["dbg_qh"].tensor, i2 * 128 * S, [[S, 128], [1, S]]),
                   qh[i2][:, :], waits=[t_attn_T])
            em.dma("sync", bass.AP(dbg["dbg_kh"].tensor, i2 * 128 * S, [[S, 128], [1, S]]),
                   kh[i2][:, :], waits=[t_attn_T])
            em.dma("sync", bass.AP(dbg["dbg_attnoutT"].tensor, i2 * 128 * S, [[S, 128], [1, S]]),
                   aoT[i2][:, :], waits=[t_attn_T, t_attn_V])
        for tt in range(NT):
            em.dma("sync", bass.AP(dbg["dbg_vaug"].tensor, tt * 128 * PAIR * 8,
                                   [[PAIR * 8, 128], [1, PAIR * 8]]),
                   vaug[tt][:, :], waits=[t_attn_T])

    if _HLIM < H:
        em.emit("sync", lambda e: e.wait_ge(sems["dS"], em.cnt["dS"]))
        nc._dbg_off = dict(o_aoT=o_aoT, o_zsb=o_zsb, o_qh=o_qh, o_kh=o_kh,
                           o_vaug_t=o_vaug_t, o_attnU=o_attnU, o_woT=o_woT, ARW=ARW,
                           o_x2hiT=0, o_x2loT=0, o_xst=0, o_xmid=0, o_x2r=0)
        with nc.Block() as block:
            def runner0(eng_name):
                def go(eng):
                    for w, fn, inc in em.ops[eng_name]:
                        for s2, v2 in w:
                            eng.wait_ge(sems[s2], v2)
                        ins = fn(eng)
                        if inc is not None:
                            s2 = inc[0]
                            ins.then_inc(sems[s2], 16 if s2.startswith("d") else 1)
                return go
            block.sync(runner0("sync"))
            block.gpsimd(runner0("gpsimd"))
            block.tensor(runner0("tensor"))
            block.vector(runner0("vector"))
            block.scalar(runner0("scalar"))
        for g2 in reversed(ctx):
            g2.__exit__(None, None, None)
        return nc

    t_dump = ("dS", em.cnt.get("dS", 0)) if debug else None
    ar.freep("qh")
    ar.freep("kh")
    for tt in range(NT):
        ar.freep(f"vaug{tt}")
    ar.freep("attnU")
    ar.freep("zsb")

    o_x2hiT = ar.alloc("x2hiT", NDC * S * 2)
    o_x2loT = ar.alloc("x2loT", NDC * S * 2)
    x2hiT = [bf(o_x2hiT + i2 * S * 2, S) for i2 in range(NDC)]
    x2loT = [bf(o_x2loT + i2 * S * 2, S) for i2 in range(NDC)]
    o_xst = ar.alloc("xst", 3 * 1024 * 4)
    xst = [f32(o_xst + k * 4096, 1024) for k in range(3)]
    o_xmid = ar.alloc("xmid", 3 * 1024 * 4)
    xmid_r = [f32(o_xmid + k * 4096, 1024) for k in range(3)]
    o_x2r = ar.alloc("x2r", 3 * 1024 * 4)
    x2_r = [f32(o_x2r + k * 4096, 1024) for k in range(3)]
    o_xhl = ar.alloc("xhl", 6 * 1024 * 2)
    x2hi_r = [bf(o_xhl + k * 2048, 1024) for k in range(3)]
    x2lo_r = [bf(o_xhl + (3 + k) * 2048, 1024) for k in range(3)]
    o_sq = ar.alloc("sqdump", 3 * 1024 * 4)
    sqd = [f32(o_sq + k * 4096, 1024) for k in range(3)]

    xst_user = [t_dump or t_attn_T, t_dump or t_attn_T, t_dump or t_attn_T]
    xmid_user = [t_dump or t_attn_T] * 3
    x2_user = [[t_dump or t_attn_T], [t_dump or t_attn_T], [t_dump or t_attn_T]]
    xhl_user = [t_dump or t_attn_T] * 3
    sq_user = [t_dump or t_cst] * 3
    ln_guard = [t_cst, t_cst, t_cst]
    spill_toks = [None] * NT
    w_toks = [None] * NT
    hilo_toks = [None] * NT
    last_wo_tok = [None]
    tok_hilo = None
    t_wo_ev = None

    def emit_xld(tt2):
        if tt2 < NT:
            em.dma("scalar", xst[tt2 % 3][:, :],
                   bass.AP(x_d.tensor, tt2 * 128 * D, [[D, 128], [1, D]]),
                   waits=[xst_user[tt2 % 3], t_attn_V], sem="dX")

    def ln1_gen(tt):
        g = tt % 2
        emit_xld(tt + 2)
        for ncol in range(2):
            for dc in range(NDC):
                def mkwo(t, tt=tt, ncol=ncol, dc=dc, g=g):
                    return t.matmul(pS[g][:, ncol * 512:(ncol + 1) * 512],
                                    aoT[dc][:, tt * 128:(tt + 1) * 128],
                                    woT[dc][:, ncol * 512:(ncol + 1) * 512],
                                    start=(dc == 0), stop=(dc == NDC - 1))
                w = []
                if ncol == 0 and dc == 0:
                    w = [pS_ready[g], wo_done[0], t_attn_V]
                em.emit("tensor", mkwo, waits=w)
        tok_wo = em.bump("tensor", "sT")
        if tt == NT - 1:
            last_wo_tok[0] = tok_wo
        xm = xmid_r[tt % 3]
        em.emit("vector", lambda e, xm=xm, g=g, tt=tt: e.tensor_tensor(
            xm, pS[g][:, 0:1024], xst[tt % 3][:, :], ALU.add),
            waits=[tok_wo, ("dX", 16 * min(tt + 3, NT)), xmid_user[tt % 3]])
        tok_xmid = em.bump("vector", "sV")
        pS_ready[g] = tok_xmid
        xst_user[tt % 3] = tok_xmid
        if debug:
            em.dma("sync", bass.AP(dbg["dbg_xmid"].tensor, tt * 128 * D, [[D, 128], [1, D]]),
                   xm, waits=[tok_xmid])
        yield
        ls = lnsc[tt % 3]
        em.emit("vector", lambda e, xm=xm, ls=ls: e.reduce_sum(ls[:, 0:1], xm, axis=AX.X),
                waits=[ln_guard[tt % 3]])
        em.emit("scalar", lambda e, xm=xm, ls=ls, tt=tt: e.activation(
            sqd[tt % 3], xm, AF.Square, bias=CST[0.0], scale=CST[1.0]),
            waits=[tok_xmid, sq_user[tt % 3], t_cst], inc=("sS", 1))
        tok_sq = em.snap("sS")
        em.emit("vector", lambda e, ls=ls, tt=tt: e.reduce_sum(ls[:, 1:2], sqd[tt % 3], axis=AX.X),
                waits=[tok_sq])
        sq_user[tt % 3] = em.bump("vector", "sV")
        em.emit("vector", lambda e, ls=ls: e.tensor_scalar(ls[:, 2:3], ls[:, 0:1], CST[1.0 / D],
                                                           None, ALU.mult))
        em.emit("vector", lambda e, ls=ls: e.tensor_scalar(ls[:, 3:4], ls[:, 1:2], CST[1.0 / D],
                                                           None, ALU.mult))
        em.emit("vector", lambda e, ls=ls: e.tensor_tensor(ls[:, 4:5], ls[:, 2:3], ls[:, 2:3], ALU.mult))
        em.emit("vector", lambda e: e.tensor_copy(consts[:, 8:9], consts[:, 5:6]))
        em.emit("vector", lambda e: e.tensor_copy(consts[:, 9:10], consts[:, 5:6]))
        em.emit("vector", lambda e, ls=ls: e.tensor_tensor(ls[:, 5:6], ls[:, 3:4], ls[:, 4:5], ALU.subtract))
        tok_var = em.bump("vector", "sV")
        em.emit("scalar", lambda e, ls=ls: e.activation(ls[:, 6:7], ls[:, 5:6], AF.Ln,
                                                        bias=CST[1e-5], scale=CST[1.0]),
                waits=[tok_var])
        em.emit("scalar", lambda e, ls=ls: e.activation(ls[:, 7:8], ls[:, 6:7], AF.Exp,
                                                        bias=CST[0.0], scale=CST[-0.5]))
        tok_rstd = em.bump("scalar", "sS")
        yield
        em.emit("vector", lambda e, ls=ls: e.scalar_tensor_tensor(
            ls[:, 8:9], ls[:, 2:3], CST[-1.0], ls[:, 7:8], ALU.mult, ALU.mult), waits=[tok_rstd])
        tok_nmr = em.bump("vector", "sV")
        x2t = x2_r[tt % 3]
        em.emit("scalar", lambda e, xm=xm, x2t=x2t, ls=ls: e.activation(
            x2t, xm, AF.Identity, bias=ls[:, 8:9], scale=ls[:, 7:8]),
            waits=[tok_nmr] + x2_user[tt % 3]
                  + ([("dSP", 16 * (tt - 2))] if tt >= 3 else []))
        tok_x2 = em.bump("scalar", "sS")
        ln_guard[tt % 3] = tok_x2
        xmid_user[tt % 3] = tok_x2
        spill_toks[tt] = em.dma("sync", bass.AP(x2_spill.tensor, tt * 128 * D, [[D, 128], [1, D]]),
                                x2t, waits=[tok_x2], sem="dSP")
        if debug:
            em.dma("sync", bass.AP(dbg["dbg_x2"].tensor, tt * 128 * D, [[D, 128], [1, D]]),
                   x2t, waits=[tok_x2])
        yield
        hi = x2hi_r[tt % 3]
        lo = x2lo_r[tt % 3]
        em.emit("scalar", lambda e, hi=hi, x2t=x2t: e.copy(hi, x2t), waits=[xhl_user[tt % 3]])
        tok_hi = em.bump("scalar", "sS")
        em.emit("vector", lambda e, lo=lo, x2t=x2t, hi=hi: e.tensor_tensor(lo, x2t, hi, ALU.subtract),
                waits=[tok_hi])
        tok_lo = em.bump("vector", "sV")
        x2_user[tt % 3] = [tok_lo]
        for dc in range(NDC):
            em.emit("tensor", lambda t, dc=dc, hi=hi: t.transpose(
                pAV_bf[0][:, dc * 128:(dc + 1) * 128], hi[:, dc * 128:(dc + 1) * 128], ident[:, :]),
                waits=[tok_hi, pAV_user[0]] if dc == 0 else [])
        for dc in range(NDC):
            em.emit("tensor", lambda t, dc=dc, lo=lo: t.transpose(
                pAV_bf[1][:, dc * 128:(dc + 1) * 128], lo[:, dc * 128:(dc + 1) * 128],
                ident[:, :]), waits=[tok_lo, pAV_user[1]] if dc == 0 else [])
        tok_tr = em.bump("tensor", "sT")

        def ev_hi(e, tt=tt):
            dsth = bf_ap(o_x2hiT + tt * 256, [[S, NDC], [1, 128]])
            return e.tensor_copy(dsth, bass.AP(pAV_bf[0], 0, [[1024, 128], [128, NDC], [1, 128]]))
        em.emit("vector", ev_hi, waits=[tok_tr])

        def ev_lo(e, tt=tt):
            dstl = bf_ap(o_x2loT + tt * 256, [[S, NDC], [1, 128]])
            return e.tensor_copy(dstl, bass.AP(pAV_bf[1], 0, [[1024, 128], [128, NDC], [1, 128]]))
        em.emit("vector", ev_lo)
        tok_hilo = em.bump("vector", "sV")
        pAV_user[0] = tok_hilo
        pAV_user[1] = tok_hilo
        xhl_user[tt % 3] = tok_hilo
        hilo_toks[tt] = tok_hilo
        yield

    o_lgt = ar.alloc("lgt", 512 * 4)
    lgt = f32(o_lgt, 512)
    lgt_user = [None]
    tok_pex_prev = [None]
    last_rt_tok = [None]

    def emit_router_chunk(ch):
        for ph, (lhsl, rh) in enumerate(((x2hiT, wr_hi), (x2hiT, wr_lo), (x2loT, wr_hi))):
            for dc in range(NDC):
                def mkr(t, ph=ph, dc=dc, lhsl=lhsl, rh=rh, ch=ch):
                    return t.matmul(pZR[0:8, 0:512], rh[:, dc * 8:(dc + 1) * 8],
                                    lhsl[dc][:, ch * 512:(ch + 1) * 512],
                                    start=(ph == 0 and dc == 0), stop=(ph == 2 and dc == NDC - 1))
                w = []
                if ph == 0 and dc == 0:
                    w = [hilo_toks[4 * ch + 3], pZR_user[0], pZR_user[1], t_wrlo]
                em.emit("tensor", mkr, waits=w)
        tok_rt = em.bump("tensor", "sT")
        last_rt_tok[0] = tok_rt
        em.emit("vector", lambda e: e.tensor_copy(lgt[0:8, :], pZR[0:8, 0:512]),
                waits=[tok_rt, lgt_user[0]])
        tok_le = em.bump("vector", "sV")
        pZR_user[0] = tok_le
        for j in range(4):
            em.emit("tensor", lambda t, j=j: t.transpose(pB[:, 8 * j:8 * j + 8],
                                                         lgt[0:8, j * 128:(j + 1) * 128],
                                                         ident_f[0:8, 0:8]),
                    waits=([tok_le, pB_user[0], pB_user[1], t_idf] if j == 0 else []))
        tok_tb = em.bump("tensor", "sT")
        lgt_user[0] = tok_tb
        for j in range(4):
            tt = 4 * ch + j
            base = 8 * j
            lg_ap = pB[:, base:base + 8]
            if debug:
                sl_d = (tt % 2) * 32
                em.emit("vector", lambda e, sl_d=sl_d, lg_ap=lg_ap: e.tensor_copy(
                    lgdump[:, sl_d:sl_d + 8], lg_ap), waits=[tok_tb, lgd_user[tt % 2]])
                tok_lgd = em.bump("vector", "sV")
                lgd_user[tt % 2] = em.dma("sync",
                                          bass.AP(dbg["dbg_lg"].tensor, tt * 128 * NE, [[NE, 128], [1, NE]]),
                                          lgdump[:, sl_d:sl_d + 8], waits=[tok_lgd])
            em.emit("vector", lambda e, lg_ap=lg_ap: e.reduce_max(r8[:, 8:9], lg_ap,
                                                                  axis=AX.X, negate=True),
                    waits=[tok_tb, tok_pex_prev[0]])
            tok_mx = em.bump("vector", "sV")
            em.emit("scalar", lambda e, lg_ap=lg_ap: e.activation(r8[:, 0:8], lg_ap, AF.Exp,
                                                                  bias=r8[:, 8:9], scale=CST[1.0]),
                    waits=[tok_mx])
            tok_pex = em.bump("scalar", "sS")
            tok_pex_prev[0] = tok_pex
            em.emit("vector", lambda e: e.reduce_sum(r8[:, 10:11], r8[:, 0:8], axis=AX.X), waits=[tok_pex])
            em.emit("vector", lambda e: e.reduce_max(r8[:, 12:13], r8[:, 0:8], axis=AX.X))
            em.emit("vector", lambda e: e.reciprocal(r8[:, 11:12], r8[:, 10:11]))
            if debug:
                sl_e = (tt % 2) * 32 + 16
                em.emit("vector", lambda e, sl_e=sl_e: e.tensor_copy(lgdump[:, sl_e:sl_e + 16], r8[:, 0:16]),
                        waits=[lgd_user[2 + tt % 2]])
                tok_r8d = em.bump("vector", "sV")
                lgd_user[2 + tt % 2] = em.dma("sync",
                                              bass.AP(dbg["dbg_r8"].tensor, tt * 128 * 16, [[16, 128], [1, 16]]),
                                              lgdump[:, sl_e:sl_e + 16], waits=[tok_r8d])
            em.emit("vector", lambda e: e.tensor_scalar(r8[:, 16:24], r8[:, 0:8], r8[:, 12:13], None, ALU.is_lt))
            em.emit("vector", lambda e: e.tensor_scalar(r8[:, 24:32], r8[:, 0:8], r8[:, 11:12], None, ALU.mult))
            em.emit("vector", lambda e: e.tensor_tensor(r8[:, 32:40], r8[:, 0:8], r8[:, 16:24], ALU.mult))
            em.emit("vector", lambda e: e.tensor_copy(consts[:, 8:9], consts[:, 5:6]))
            em.emit("vector", lambda e: e.reduce_max(r8[:, 13:14], r8[:, 32:40], axis=AX.X))
            em.emit("vector", lambda e: e.tensor_copy(consts[:, 9:10], consts[:, 5:6]))
            em.emit("vector", lambda e: e.tensor_scalar(r8[:, 40:48], r8[:, 0:8], r8[:, 13:14], None, ALU.is_ge))
            em.emit("vector", lambda e: e.tensor_copy(consts[:, 8:9], consts[:, 5:6]))
            em.emit("vector", lambda e, tt=tt: e.tensor_tensor(w_sb[:, tt * 8:(tt + 1) * 8],
                                                               r8[:, 24:32], r8[:, 40:48], ALU.mult))
            w_toks[tt] = em.bump("vector", "sV")
            if debug:
                em.dma("sync", bass.AP(dbg["dbg_w"].tensor, tt * 128 * NE, [[NE, 128], [1, NE]]),
                       w_sb[:, tt * 8:(tt + 1) * 8], waits=[w_toks[tt]])
        pB_user[0] = tok_pex_prev[0]
        pB_user[1] = tok_pex_prev[0]

    ln1q = []

    def ln1_step():
        for g2 in ln1q[:]:
            try:
                next(g2)
            except StopIteration:
                ln1q.remove(g2)

    emit_xld(0)
    emit_xld(1)
    rch = 0
    for tt in range(NT + 4):
        ln1_step()
        if tt < NT:
            gg = ln1_gen(tt)
            next(gg)
            ln1q.append(gg)
        while rch < 4 and hilo_toks[4 * rch + 3] is not None:
            emit_router_chunk(rch)
            rch += 1
    assert not ln1q and rch == 4

    if debug:
        for i2 in range(NDC):
            em.dma("sync", bass.AP(dbg["dbg_x2loT"].tensor, i2 * 128 * S, [[S, 128], [1, S]]),
                   x2loT[i2][:, :], waits=[w_toks[-1]])
    t_ln_T = em.snap("sT")
    t_ln_S = em.snap("sS")
    ar.freep("attnoutT")
    ar.freep("woT")
    ar.freep("x2loT")
    ar.freep("xst")
    ar.freep("xmid")
    ar.freep("x2r")
    ar.freep("xhl")

    o_h = ar.alloc("hsb", 2 * NDC * 1024 * 2)
    h_sb = [[bf(o_h + (b * NDC + jc) * 2048, 1024) for jc in range(NDC)] for b in range(2)]
    o_wm = ar.alloc("wm", 2 * NDC * 1024 * 2)
    wm_sb = [[bf(o_wm + (b * NDC + jc) * 2048, 1024) for jc in range(NDC)] for b in range(2)]
    o_wi = ar.alloc("wi", 2 * NDC * 1024 * 2)
    wi_sb = [[bf(o_wi + (b * NDC + dc) * 2048, 1024) for dc in range(NDC)] for b in range(2)]
    o_oacc_t = [ar.alloc(f"oacc{t2}", 1024 * 4) for t2 in range(8)]
    oacc = [f32(o_oacc_t[t2], 1024) for t2 in range(8)]
    o_x2b = ar.alloc("x2back", 2 * 1024 * 4)
    x2b = [f32(o_x2b + k * 4096, 1024) for k in range(2)]
    o_outr = ar.alloc("outr", 2 * 1024 * 4)
    outr = [f32(o_outr + k * 4096, 1024) for k in range(2)]

    pool_guard = [t_ln_T, t_ln_S, w_toks[-1], spill_toks[-1]]
    if debug:
        pool_guard.append(("dS", em.cnt.get("dS", 0)))
    wi_guard = [last_wo_tok[0]]
    if debug:
        wi_guard = wi_guard + [("dS", em.cnt.get("dS", 0))]
    wi_user = [wi_guard, wi_guard]
    wm_user = [pool_guard, pool_guard]
    h_guard = [last_rt_tok[0]]
    if debug:
        h_guard = h_guard + [("dS", em.cnt.get("dS", 0))]
    h_user = [h_guard, h_guard]
    s1ev_user = [pAV_user[0], pAV_user[1]]
    oacc_user = [pool_guard] * 8
    x2b_user = [pool_guard, pool_guard]
    outr_user = [pool_guard, pool_guard]
    store_toks = []

    def _wl(u):
        return u if isinstance(u, list) else [u]

    def load_expert(e_i, buf):
        t1 = em.dma("gpsimd", bf_ap(o_wi + buf * NDC * 2048, [[1024, NDC], [1, 1024]]),
                    bass.AP(wi_d.tensor, e_i * D * D, [[D, 128], [128 * D, NDC], [1, D]]),
                    waits=_wl(wi_user[buf]))
        t2_ = em.dma("gpsimd", bf_ap(o_wm + buf * NDC * 2048, [[1024, NDC], [1, 1024]]),
                     bass.AP(wm_d.tensor, e_i * D * D, [[D, 128], [128 * D, NDC], [1, D]]),
                     waits=_wl(wm_user[buf]))
        return (t1, t2_)

    ld_toks = {(0, 0): load_expert(0, 0)}
    for half in range(2):
        def ln2_gen(t2, half=half):
            tt_g = half * 8 + t2
            xm2 = oacc[t2]
            wadd = _wl(oacc_user[t2])
            if debug:
                td = em.dma("sync", bass.AP(dbg["dbg_oacc"].tensor, tt_g * 128 * D,
                                            [[D, 128], [1, D]]),
                            oacc[t2], waits=_wl(oacc_user[t2]))
                wadd = wadd + [td]
            t_x2ld = em.dma("sync", x2b[t2 % 2][:, :],
                            bass.AP(x2_spill.tensor, tt_g * 128 * D, [[D, 128], [1, D]]),
                            waits=_wl(x2b_user[t2 % 2]) + [("dSP", 16 * NT)], sem="dB")
            em.emit("vector", lambda e2, t2=t2, xm2=xm2: e2.tensor_tensor(
                xm2, x2b[t2 % 2], xm2, ALU.add), waits=[t_x2ld] + wadd)
            tok_xm2 = em.bump("vector", "sV")
            x2b_user[t2 % 2] = tok_xm2
            oacc_user[t2] = tok_xm2
            ls = lnsc[t2 % 3]
            em.emit("vector", lambda e2, xm2=xm2, ls=ls: e2.reduce_sum(ls[:, 0:1], xm2, axis=AX.X),
                    waits=[ln_guard[t2 % 3]])
            tok_c = em.bump("vector", "sV")
            yield
            em.emit("scalar", lambda e2, xm2=xm2, t2=t2: e2.activation(
                sqd[t2 % 3], xm2, AF.Square, bias=CST[0.0], scale=CST[1.0]),
                waits=[tok_c, sq_user[t2 % 3]], inc=("sS", 1))
            tok_sq = em.snap("sS")
            em.emit("vector", lambda e2, ls=ls, t2=t2: e2.reduce_sum(ls[:, 1:2], sqd[t2 % 3], axis=AX.X),
                    waits=[tok_sq])
            sq_user[t2 % 3] = em.bump("vector", "sV")
            em.emit("vector", lambda e2, ls=ls: e2.tensor_scalar(ls[:, 2:3], ls[:, 0:1], CST[1.0 / D],
                                                                 None, ALU.mult))
            em.emit("vector", lambda e2, ls=ls: e2.tensor_scalar(ls[:, 3:4], ls[:, 1:2], CST[1.0 / D],
                                                                 None, ALU.mult))
            em.emit("vector", lambda e2, ls=ls: e2.tensor_tensor(ls[:, 4:5], ls[:, 2:3], ls[:, 2:3], ALU.mult))
            em.emit("vector", lambda e2: e2.tensor_copy(consts[:, 8:9], consts[:, 5:6]))
            em.emit("vector", lambda e2: e2.tensor_copy(consts[:, 9:10], consts[:, 5:6]))
            em.emit("vector", lambda e2, ls=ls: e2.tensor_tensor(ls[:, 5:6], ls[:, 3:4], ls[:, 4:5], ALU.subtract))
            tok_var = em.bump("vector", "sV")
            em.emit("scalar", lambda e2, ls=ls: e2.activation(ls[:, 6:7], ls[:, 5:6], AF.Ln,
                                                              bias=CST[1e-5], scale=CST[1.0]),
                    waits=[tok_var])
            em.emit("scalar", lambda e2, ls=ls: e2.activation(ls[:, 7:8], ls[:, 6:7], AF.Exp,
                                                              bias=CST[0.0], scale=CST[-0.5]))
            tok_rstd = em.bump("scalar", "sS")
            yield
            em.emit("vector", lambda e2, ls=ls: e2.scalar_tensor_tensor(
                ls[:, 8:9], ls[:, 2:3], CST[-1.0], ls[:, 7:8], ALU.mult, ALU.mult), waits=[tok_rstd])
            tok_nmr = em.bump("vector", "sV")
            ot = outr[t2 % 2]
            em.emit("scalar", lambda e2, xm2=xm2, ot=ot, ls=ls: e2.activation(
                ot, xm2, AF.Identity, bias=ls[:, 8:9], scale=ls[:, 7:8]),
                waits=[tok_nmr] + _wl(outr_user[t2 % 2]))
            tok_out = em.bump("scalar", "sS")
            ln_guard[t2 % 3] = tok_out
            oacc_user[t2] = tok_out
            st = em.dma("sync", bass.AP(out_d.tensor, tt_g * 128 * D, [[D, 128], [1, D]]),
                        outr[t2 % 2], waits=[tok_out], sem="dO")
            outr_user[t2 % 2] = st
            store_toks.append(st)
            yield

        if half == 0:
            ln2q = []

        def ln2_step():
            for g in ln2q[:]:
                try:
                    next(g)
                except StopIteration:
                    ln2q.remove(g)

        def ln2_push(t2):
            g = ln2_gen(t2)
            next(g)
            ln2q.append(g)
        for e_i in range(NE):
            buf = e_i % 2
            nxt = (half, e_i + 1) if e_i + 1 < NE else ((1, 0) if half == 0 else None)
            if nxt is not None and nxt not in ld_toks:
                ld_toks[nxt] = load_expert(nxt[1], nxt[1] % 2)
            tw = ld_toks[(half, e_i)]
            ev_last = None
            for jc in range(NDC):
                for tc in range(2):
                    rot = (jc * 2 + tc) % 2
                    ps = pAV[rot]
                    for dc in range(NDC):
                        def mk1(t, jc=jc, tc=tc, dc=dc, ps=ps, buf=buf, half=half):
                            return t.matmul(ps[:, 0:512],
                                            wi_sb[buf][dc][:, jc * 128:(jc + 1) * 128],
                                            x2hiT[dc][:, half * 1024 + tc * 512: half * 1024 + (tc + 1) * 512],
                                            start=(dc == 0), stop=(dc == NDC - 1))
                        w = []
                        if dc == 0:
                            w = _wl(s1ev_user[rot]) + [tw[0]]
                            if jc == 0 and tc == 0:
                                w = w + [tok_hilo]
                        em.emit("tensor", mk1, waits=w)
                    s1_mm_last = em.bump("tensor", "sT")
                    if jc == NDC - 1 and tc == 1:
                        wi_user[buf] = s1_mm_last

                    def ev1(e2, jc=jc, tc=tc, ps=ps, buf=buf):
                        return e2.copy(h_sb[buf][jc][:, tc * 512:(tc + 1) * 512], ps[:, 0:512])
                    em.emit("scalar", ev1, waits=[s1_mm_last] + _wl(h_user[buf]))
                    ev_last = em.bump("scalar", "sS")
                    s1ev_user[rot] = ev_last
                    if e_i == 0 and ln2q:
                        ln2_step()
            while ln2q:
                ln2_step()
            for t2 in range(8):
                tt_g = half * 8 + t2
                g = t2 % 2
                for ncol in range(2):
                    for jc in range(NDC):
                        def mk2(t, t2=t2, ncol=ncol, jc=jc, g=g, buf=buf):
                            return t.matmul(pS[g][:, ncol * 512:(ncol + 1) * 512],
                                            h_sb[buf][jc][:, t2 * 128:(t2 + 1) * 128],
                                            wm_sb[buf][jc][:, ncol * 512:(ncol + 1) * 512],
                                            start=(jc == 0), stop=(jc == NDC - 1))
                        w = []
                        if ncol == 0 and jc == 0:
                            w = _wl(pS_ready[g]) + [ev_last, tw[1]]
                        em.emit("tensor", mk2, waits=w)
                s2_mm_last = em.bump("tensor", "sT")
                if t2 == 7:
                    wm_user[buf] = s2_mm_last
                    h_user[buf] = s2_mm_last
                wcol = w_sb[:, tt_g * 8 + e_i: tt_g * 8 + e_i + 1]
                if e_i == 0:
                    em.emit("vector", lambda e2, t2=t2, g=g, wcol=wcol: e2.tensor_scalar(
                        oacc[t2], pS[g][:, 0:1024], wcol, None, ALU.mult),
                        waits=[s2_mm_last, w_toks[tt_g]] + _wl(oacc_user[t2]))
                else:
                    em.emit("vector", lambda e2, t2=t2, g=g, wcol=wcol: e2.scalar_tensor_tensor(
                        oacc[t2], pS[g][:, 0:1024], wcol, oacc[t2], ALU.mult, ALU.add),
                        waits=[s2_mm_last])
                tok_o = em.bump("vector", "sV")
                pS_ready[g] = tok_o
                oacc_user[t2] = tok_o
                if e_i == NE - 1:
                    ln2_step()
                    ln2_push(t2)

        if half == 1:
            while ln2q:
                ln2_step()

    if debug:
        for i2 in range(NDC):
            em.dma("sync", bass.AP(dbg["dbg_x2hiT"].tensor, i2 * 128 * S, [[S, 128], [1, S]]),
                   x2hiT[i2][:, :], waits=[store_toks[-1]])
    em.emit("sync", lambda e: e.wait_ge(sems["dO"], em.cnt["dO"]))
    em.emit("sync", lambda e: e.wait_ge(sems["dSP"], em.cnt["dSP"]))
    if em.cnt.get("dS"):
        em.emit("sync", lambda e: e.wait_ge(sems["dS"], em.cnt["dS"]))
    em.emit("gpsimd", lambda e: e.wait_ge(sems["dG"], em.cnt["dG"]))

    with nc.Block() as block:
        def runner(eng_name):
            def go(eng):
                for w, fn, inc in em.ops[eng_name]:
                    for s2, v2 in w:
                        eng.wait_ge(sems[s2], v2)
                    ins = fn(eng)
                    if inc is not None:
                        s2 = inc[0]
                        ins.then_inc(sems[s2], 16 if s2.startswith("d") else 1)
            return go
        block.sync(runner("sync"))
        block.gpsimd(runner("gpsimd"))
        block.tensor(runner("tensor"))
        block.vector(runner("vector"))
        block.scalar(runner("scalar"))

    nc._dbg_off = dict(o_aoT=o_aoT, o_zsb=o_zsb, o_x2hiT=o_x2hiT, o_x2loT=o_x2loT,
                       o_xst=o_xst, o_xmid=o_xmid, o_x2r=o_x2r, o_qh=o_qh, o_kh=o_kh,
                       o_vaug_t=o_vaug_t, o_attnU=o_attnU, o_woT=o_woT, ARW=ARW)
    for g2 in reversed(ctx):
        g2.__exit__(None, None, None)
    return nc


_HOST_CONSTS = None


def _host_consts():
    global _HOST_CONSTS
    if _HOST_CONSTS is None:
        import ml_dtypes
        pos = np.arange(S, dtype=np.float64)
        inv = 10000.0 ** (-(2.0 * np.arange(32, dtype=np.float64)) / 64.0)
        ang = np.outer(inv, pos)
        cosT = np.tile(np.cos(ang), (4, 1)).astype(np.float32)
        sinT = np.tile(np.sin(ang), (4, 1)).astype(np.float32)
        identb = np.eye(128, dtype=np.float32).astype(ml_dtypes.bfloat16)
        k_idx = np.arange(128)[:, None]
        q_idx = np.arange(128)[None, :]
        trimask = np.where(k_idx <= q_idx, np.float32(0.0),
                           np.float32(-1e9)).astype(ml_dtypes.bfloat16)
        selm = np.zeros((16, 1024), np.float32)
        for s_i in range(16):
            selm[s_i, 64 * s_i:64 * s_i + 64] = 1.0
        selm = selm.astype(ml_dtypes.bfloat16)
        _HOST_CONSTS = (np.ascontiguousarray(cosT), np.ascontiguousarray(sinT),
                        np.ascontiguousarray(identb), np.ascontiguousarray(trimask),
                        np.ascontiguousarray(selm))
    return _HOST_CONSTS


def kernel(L0_attn_x, L0_attn_wq, L0_attn_wk, L0_attn_wv, L0_attn_wo,
           L0_ffn_Wr, L0_ffn_Wi, L0_ffn_Wo):
    cosT, sinT, identb, trimask, selm = _host_consts()
    nc = build_nc(debug=False)
    shared = {
        "wq": np.ascontiguousarray(np.asarray(L0_attn_wq, dtype=np.float32)),
        "wk": np.ascontiguousarray(np.asarray(L0_attn_wk, dtype=np.float32)),
        "wv": np.ascontiguousarray(np.asarray(L0_attn_wv, dtype=np.float32)),
        "wo": np.ascontiguousarray(np.asarray(L0_attn_wo, dtype=np.float32)),
        "wr": np.ascontiguousarray(np.asarray(L0_ffn_Wr, dtype=np.float32)),
        "wi": np.ascontiguousarray(np.asarray(L0_ffn_Wi, dtype=np.float32)),
        "wm": np.ascontiguousarray(np.asarray(L0_ffn_Wo, dtype=np.float32)),
        "costab": cosT, "sintab": sinT, "identb": identb, "trimask": trimask,
        "selmat": selm,
    }
    xs = np.asarray(L0_attn_x, dtype=np.float32)
    in_maps = []
    for c in range(8):
        m = dict(shared)
        m["x"] = np.ascontiguousarray(xs[c])
        in_maps.append(m)
    res = run_bass_kernel_spmd(nc, in_maps, core_ids=list(range(8)))
    out = np.stack([np.asarray(res.results[c]["out"]) for c in range(8)], axis=0)
    return out.astype(np.float32)


# version 26
# speedup vs baseline: 1.0052x; 1.0052x over previous
import numpy as np
import concourse.bass as bass
import concourse.mybir as mybir
from concourse.bass_utils import run_bass_kernel_spmd

F32 = mybir.dt.float32
BF16 = mybir.dt.bfloat16
AF = mybir.ActivationFunctionType
ALU = mybir.AluOpType
AX = mybir.AxisListType

S = 2048
D = 1024
H = 16
NE = 8
NT = S // 128
NDC = D // 128
PAIR = 193

ENGS = ("sync", "gpsimd", "tensor", "vector", "scalar")
SEM_OWNER = {"sT": "tensor", "sV": "vector", "sS": "scalar", "sG": "gpsimd"}


class Em:
    def __init__(self):
        self.ops = {k: [] for k in ENGS}
        self.cnt = {}
        self.lastw = {k: {} for k in ENGS}

    def emit(self, eng, fn, waits=(), inc=None):
        w = []
        for t in waits:
            if t is None:
                continue
            s, v = t
            if SEM_OWNER.get(s) == eng:
                continue
            if self.lastw[eng].get(s, -1) < v:
                w.append((s, v))
                self.lastw[eng][s] = v
        tok = None
        if inc is not None:
            s, a = inc
            self.cnt[s] = self.cnt.get(s, 0) + a
            tok = (s, self.cnt[s])
        self.ops[eng].append([w, fn, inc])
        return tok

    def bump(self, eng, sem):
        op = self.ops[eng][-1]
        assert op[2] is None
        self.cnt[sem] = self.cnt.get(sem, 0) + 1
        op[2] = (sem, self.cnt[sem])
        return (sem, self.cnt[sem])

    def snap(self, sem):
        return (sem, self.cnt.get(sem, 0))

    def dma(self, eng, out_ap, in_ap, waits=(), sem=None, **kw):
        if sem is None:
            sem = "dG" if eng == "gpsimd" else "dS"
        return self.emit(eng, lambda e, o=out_ap, i=in_ap, kw=kw: e.dma_start(o, i, **kw),
                         waits=waits, inc=(sem, 16))


class Arena:
    def __init__(self, nbytes):
        self.free = [(0, nbytes)]
        self.allocs = {}

    def alloc(self, name, nbytes, align=64):
        for i, (off, sz) in enumerate(self.free):
            aoff = (off + align - 1) // align * align
            if aoff + nbytes <= off + sz:
                rep = [p for p in ((off, aoff - off),
                                   (aoff + nbytes, off + sz - (aoff + nbytes))) if p[1] > 0]
                self.free[i:i + 1] = rep
                self.allocs[name] = (aoff, nbytes)
                return aoff
        raise MemoryError(f"arena OOM for {name} ({nbytes}B); free={self.free}")

    def freep(self, name):
        off, sz = self.allocs.pop(name)
        self.free.append((off, sz))
        self.free.sort()
        m = []
        for o2, s2 in self.free:
            if m and m[-1][0] + m[-1][1] == o2:
                m[-1] = (m[-1][0], m[-1][1] + s2)
            else:
                m.append((o2, s2))
        self.free = m


def build_nc(debug=False):
    nc = bass.Bass("TRN2", target_bir_lowering=False, detect_race_conditions=False)

    x_d = nc.dram_tensor("x", [S, D], F32, kind="ExternalInput").ap()
    wq_d = nc.dram_tensor("wq", [D, D], F32, kind="ExternalInput").ap()
    wk_d = nc.dram_tensor("wk", [D, D], F32, kind="ExternalInput").ap()
    wv_d = nc.dram_tensor("wv", [D, D], F32, kind="ExternalInput").ap()
    wo_d = nc.dram_tensor("wo", [D, D], F32, kind="ExternalInput").ap()
    wr_d = nc.dram_tensor("wr", [D, NE], F32, kind="ExternalInput").ap()
    wi_d = nc.dram_tensor("wi", [NE, D, D], F32, kind="ExternalInput").ap()
    wm_d = nc.dram_tensor("wm", [NE, D, D], F32, kind="ExternalInput").ap()
    cos_d = nc.dram_tensor("costab", [128, S], F32, kind="ExternalInput").ap()
    sin_d = nc.dram_tensor("sintab", [128, S], F32, kind="ExternalInput").ap()
    id_d = nc.dram_tensor("identb", [128, 128], BF16, kind="ExternalInput").ap()
    tm_d = nc.dram_tensor("trimask", [128, 128], BF16, kind="ExternalInput").ap()
    sel_d = nc.dram_tensor("selmat", [16, 1024], BF16, kind="ExternalInput").ap()
    out_d = nc.dram_tensor("out", [S, D], F32, kind="ExternalOutput").ap()
    x2_spill = nc.dram_tensor("x2spill", [S, D], F32, kind="Internal").ap()
    dbg = {}
    if debug:
        for nm, shape, dt in [("dbg_qh", [D, S], BF16), ("dbg_kh", [D, S], BF16),
                              ("dbg_vaug", [NT * 128, PAIR * 8], BF16),
                              ("dbg_attnoutT", [D, S], BF16),
                              ("dbg_xmid", [S, D], F32), ("dbg_x2", [S, D], F32),
                              ("dbg_w", [S, NE], F32), ("dbg_oacc", [S, D], F32),
                              ("dbg_lg", [S, NE], F32), ("dbg_x2hiT", [D, S], BF16),
                              ("dbg_x2loT", [D, S], BF16), ("dbg_r8", [S, 16], F32)]:
            dbg[nm] = nc.dram_tensor(nm, shape, dt, kind="ExternalOutput").ap()

    em = Em()
    AR_BYTES = 203 * 1024 - 1024
    ar = Arena(AR_BYTES)
    ctx = []

    def enter(g):
        t = g.__enter__()
        ctx.append(g)
        return t

    arena_bf = enter(nc.sbuf_tensor("arena", [128, AR_BYTES // 2], BF16))
    arena_f32h = arena_bf.bitcast(F32)
    ident = enter(nc.sbuf_tensor("id_sb", [128, 128], BF16))
    ident_f = enter(nc.sbuf_tensor("idf_sb", [128, 128], F32))
    trimask = enter(nc.sbuf_tensor("tm_sb", [128, 128], BF16))
    consts = enter(nc.sbuf_tensor("consts", [128, 16], F32))
    selmat = enter(nc.sbuf_tensor("selmat_sb", [16, 1024], BF16))
    w_sb = enter(nc.sbuf_tensor("w_sb", [128, NT * NE], F32))
    wr_f = enter(nc.sbuf_tensor("wr_f", [128, NDC * NE], F32))
    wr_hi = enter(nc.sbuf_tensor("wr_hi", [128, NDC * NE], BF16))
    wr_lo = enter(nc.sbuf_tensor("wr_lo", [128, NDC * NE], BF16))
    rsb = enter(nc.sbuf_tensor("rsb", [128, 16], F32))
    rrow = enter(nc.sbuf_tensor("rrow", [16, 256], BF16))
    lnsc = [enter(nc.sbuf_tensor(f"lnsc{i}", [128, 16], F32)) for i in range(3)]
    r8 = enter(nc.sbuf_tensor("r8", [128, 48], F32))
    lgdump = enter(nc.sbuf_tensor("lgdump", [128, 64], F32)) if debug else None
    lgd_user = [None, None, None, None]

    pS = [enter(nc.psum_tensor(f"pS{i}", [128, 1024], F32)) for i in range(2)]
    pAV = [enter(nc.psum_tensor(f"pAV{i}", [128, 512], F32)) for i in range(2)]
    pZR = enter(nc.psum_tensor("pZR", [128, 512], F32))
    pB = enter(nc.psum_tensor("pB", [128, 512], F32))
    pS_bf = [p.bitcast(BF16) for p in pS]
    pAV_bf = [p.bitcast(BF16) for p in pAV]

    sems = {s: enter(nc.semaphore(s)) for s in ("dG", "dGC", "dS", "dC", "dR", "dW0", "dW1", "dX", "dSP", "dB", "dO", "sT", "sV", "sS")}

    CONST_VALS = {0: 0.125, 1: -20.0, 2: 1e-5, 3: -0.5, 4: 0.0, 5: 1.0,
                  6: 1.0 / D, 7: -1.0}
    CST = {v: consts[:, i:i + 1] for i, v in CONST_VALS.items()}

    ARW = AR_BYTES // 2

    def bf(off, cols):
        assert off % 2 == 0
        return arena_bf[:, off // 2: off // 2 + cols]

    def f32(off, cols):
        assert off % 4 == 0
        return arena_f32h[:, off // 4: off // 4 + cols]

    def bf_ap(off_bytes, dims):
        return bass.AP(arena_bf, off_bytes // 2, [[ARW, 128]] + dims)

    for i, v in CONST_VALS.items():
        em.emit("vector", lambda e, i=i, v=v: e.memset(consts[:, i:i + 1], v))
    em.emit("vector", lambda e: e.memset(r8[:, 8:48], 0.0))
    t_cst = em.bump("vector", "sV")
    t_ones = em.dma("sync", selmat[:, :], sel_d[:, :], sem="dC")
    t_id = em.dma("sync", ident[:, :], id_d[:, :], sem="dC")
    t_idf = em.dma("gpsimd", ident_f[:, :], id_d[:, :], sem="dGC")
    t_tm = em.dma("sync", trimask[:, :], tm_d[:, :], sem="dC")

    o_cos = ar.alloc("cos", S * 4)
    o_sin = ar.alloc("sin", S * 4)
    cosT = f32(o_cos, S)
    sinT = f32(o_sin, S)
    t_cos = em.dma("sync", cosT, cos_d[:, :], sem="dC")
    t_sin = em.dma("sync", sinT, sin_d[:, :], sem="dC")

    t_wrf = em.dma("sync", wr_f[:, :],
                   bass.AP(wr_d.tensor, 0, [[NE, 128], [NE * 128, NDC], [1, NE]]), sem="dC")
    t_const = ("dC", 96)
    t_wrhi = em.emit("vector", lambda e: e.tensor_copy(wr_hi[:, :], wr_f[:, :]),
                     waits=[t_const], inc=("sV", 1))
    em.emit("vector", lambda e: e.tensor_copy(consts[:, 8:9], consts[:, 5:6]))
    em.emit("vector", lambda e: e.tensor_copy(consts[:, 9:10], consts[:, 5:6]))
    t_wrlo = em.emit("vector",
                     lambda e: e.tensor_tensor(wr_lo[:, :], wr_f[:, :], wr_hi[:, :], ALU.subtract),
                     waits=[t_wrhi], inc=("sV", 1))

    o_xb = ar.alloc("xb", NT * 1024 * 2)
    xb = [bf(o_xb + tt * 2048, 1024) for tt in range(NT)]
    t_xp = [em.dma("gpsimd", bf_ap(o_xb + q4 * 4 * 2048, [[1024, 4], [1, 1024]]),
                   bass.AP(x_d.tensor, q4 * 4 * 128 * D, [[D, 128], [128 * D, 4], [1, D]]))
            for q4 in range(4)]
    t_x = t_xp[-1]

    o_xT = ar.alloc("xT", NDC * S * 2)
    xT = [bf(o_xT + dc * S * 2, S) for dc in range(NDC)]

    pS_ready = [t_x, t_x]
    tok_xT_last = None
    for dc in range(NDC):
        g = dc % 2
        for tt in range(NT):
            half, sl = tt // 8, tt % 8

            def mk(t, dc=dc, tt=tt, g=g, half=half, sl=sl):
                return t.transpose(pS_bf[g][:, half * 1024 + sl * 128: half * 1024 + (sl + 1) * 128],
                                   xb[tt][:, dc * 128:(dc + 1) * 128], ident[:, :])
            w = [t_xp[min(tt // 4, 3)], t_const, pS_ready[g]] if tt % 4 == 0 else []
            em.emit("tensor", mk, waits=w)
        mmtok = em.bump("tensor", "sT")
        tok_xT_last = em.emit("vector",
                              lambda e, dc=dc, g=g: e.tensor_copy(xT[dc][:, :], pS_bf[g][:, 0:2048]),
                              waits=[mmtok], inc=("sV", 1))
        pS_ready[g] = tok_xT_last
    ar.freep("xb")

    o_qh = ar.alloc("qh", NDC * S * 2)
    o_kh = ar.alloc("kh", NDC * S * 2)
    qh = [bf(o_qh + i * S * 2, S) for i in range(NDC)]
    kh = [bf(o_kh + i * S * 2, S) for i in range(NDC)]
    o_vaug_t = [ar.alloc(f"vaug{tt}", PAIR * 8 * 2, align=128) for tt in range(NT)]
    vaug = [bf(o_vaug_t[tt], PAIR * 8) for tt in range(NT)]

    o_wstg = ar.alloc("wstg", 2 * 1024 * 2)
    wstg = [bf(o_wstg + s * 2048, 1024) for s in range(2)]
    stg_user = [t_x, t_x]

    def load_transpose_weight(wd, o_dstT, permuted, extra=()):
        last = None
        for i in range(NDC):
            s = i % 2
            ld = None
            if permuted:
                c, par = i // 2, i % 2
                for j in range(4):
                    hh = 4 * c + j
                    base = 64 * hh + par
                    src = bass.AP(wd.tensor, base * D, [[2 * D, 32], [1, D]])
                    ld = em.dma("gpsimd", wstg[s][32 * j:32 * j + 32, :], src, sem=f"dW{s}",
                                waits=([stg_user[s]] + list(extra)) if j == 0 else [])
            else:
                src = bass.AP(wd.tensor, i * 128 * D, [[D, 128], [1, D]])
                ld = em.dma("gpsimd", wstg[s][:, :], src, sem=f"dW{s}",
                            waits=[stg_user[s]] + list(extra))
            g = i % 2
            for dc in range(NDC):
                def mk(t, dc=dc, s=s, g=g):
                    return t.transpose(pS_bf[g][:, dc * 128:(dc + 1) * 128],
                                       wstg[s][:, dc * 128:(dc + 1) * 128], ident[:, :])
                w = [pS_ready[g], ld, t_const] if dc == 0 else []
                em.emit("tensor", mk, waits=w)
            mmtok = em.bump("tensor", "sT")
            stg_user[s] = mmtok

            def ev(e, i=i, g=g, o_dstT=o_dstT):
                dst = bf_ap(o_dstT + i * 256, [[1024, NDC], [1, 128]])
                return e.tensor_copy(dst, pS_bf[g][:, 0:1024])
            last = em.emit("vector", ev, waits=[mmtok], inc=("sV", 1))
            pS_ready[g] = last
        return last

    o_rope = ar.alloc("rope", 2 * S * 2)
    ropeEO = [[bf(o_rope + p * S * 2, S) for p in range(2)]]
    o_rtmp = ar.alloc("rtmp", 2 * 512 * 4)
    rtmp = [f32(o_rtmp + i * 2048, 512) for i in range(2)]

    proj_ps = [pAV[0], pAV[1], pZR, pB]
    proj_reader = [tok_xT_last] * 4
    rope_war = [None]

    def proj_qk(wT, wt_last, dst_heads):
        repack_last = None
        rp_toks = []
        for c in range(4):
            rope_done = None
            for tc in range(4):
                rot = tc % 2
                psE, psO = proj_ps[2 * rot], proj_ps[2 * rot + 1]
                for dc in range(NDC):
                    def mkE(t, c=c, tc=tc, dc=dc, psE=psE, wT=wT):
                        return t.matmul(psE[:, 0:512],
                                        wT[dc][:, (2 * c) * 128:(2 * c + 1) * 128],
                                        xT[dc][:, tc * 512:(tc + 1) * 512],
                                        start=(dc == 0), stop=(dc == NDC - 1))
                    w = [proj_reader[2 * rot], wt_last] if dc == 0 else []
                    em.emit("tensor", mkE, waits=w)
                mmE = em.bump("tensor", "sT")
                for dc in range(NDC):
                    def mkO(t, c=c, tc=tc, dc=dc, psO=psO, wT=wT):
                        return t.matmul(psO[:, 0:512],
                                        wT[dc][:, (2 * c + 1) * 128:(2 * c + 2) * 128],
                                        xT[dc][:, tc * 512:(tc + 1) * 512],
                                        start=(dc == 0), stop=(dc == NDC - 1))
                    w = [proj_reader[2 * rot + 1]] if dc == 0 else []
                    em.emit("tensor", mkO, waits=w)
                mmO = em.bump("tensor", "sT")
                cs = cosT[:, tc * 512:(tc + 1) * 512]
                sn = sinT[:, tc * 512:(tc + 1) * 512]
                eE, eO = psE[:, 0:512], psO[:, 0:512]
                oE = ropeEO[0][0][:, tc * 512:(tc + 1) * 512]
                oO = ropeEO[0][1][:, tc * 512:(tc + 1) * 512]
                w0 = [mmE, mmO, t_const]
                if rope_war[0] is not None and tc == 0:
                    w0 = w0 + [rope_war[0]]
                em.emit("vector", lambda e, eE=eE, cs=cs: e.tensor_tensor(rtmp[0], eE, cs, ALU.mult),
                        waits=w0)
                em.emit("vector", lambda e, eO=eO, sn=sn: e.tensor_tensor(rtmp[1], eO, sn, ALU.mult))
                em.emit("vector", lambda e, oE=oE: e.tensor_tensor(oE, rtmp[0], rtmp[1], ALU.subtract))
                em.emit("vector", lambda e, eE=eE, sn=sn: e.tensor_tensor(rtmp[0], eE, sn, ALU.mult))
                em.emit("vector", lambda e, eO=eO, cs=cs: e.tensor_tensor(rtmp[1], eO, cs, ALU.mult))
                rope_done = em.emit("vector",
                                    lambda e, oO=oO: e.tensor_tensor(oO, rtmp[0], rtmp[1], ALU.add),
                                    inc=("sV", 1))
                proj_reader[2 * rot] = rope_done
                proj_reader[2 * rot + 1] = rope_done
            for j in range(4):
                hh = 4 * c + j
                i, pbase = hh // 2, (hh % 2) * 64
                em.dma("sync", dst_heads[i][pbase:pbase + 32, :],
                       ropeEO[0][0][32 * j:32 * j + 32, :], waits=[rope_done], sem="dR")
                repack_last = em.dma("sync", dst_heads[i][pbase + 32:pbase + 64, :],
                                     ropeEO[0][1][32 * j:32 * j + 32, :], sem="dR")
            rope_war[0] = repack_last
            rp_toks.append(repack_last)
        return repack_last, rp_toks

    o_wT = ar.alloc("wT", NDC * 1024 * 2)
    wT_tiles = [bf(o_wT + i * 1024 * 2, 1024) for i in range(NDC)]
    tq_last = load_transpose_weight(wq_d, o_wT, True)
    q_ready, q_rp = proj_qk(wT_tiles, tq_last, qh)
    t_qproj_T = em.snap("sT")
    tk_last = load_transpose_weight(wk_d, o_wT, True, extra=[t_qproj_T])
    k_ready, k_rp = proj_qk(wT_tiles, tk_last, kh)
    t_kproj_T = em.snap("sT")
    ar.freep("rope")
    ar.freep("rtmp")
    ar.freep("cos")
    ar.freep("sin")
    vfill_last = None
    for tt in range(NT):
        em.emit("vector", lambda e, tt=tt: e.memset(
            bf_ap(o_vaug_t[tt] + 66 * 2, [[PAIR, 8], [1, 63]]), 0.0),
            waits=[k_ready] if tt == 0 else [])
        vfill_last = em.emit("vector", lambda e, tt=tt: e.memset(
            bf_ap(o_vaug_t[tt] + 64 * 2, [[PAIR, 8], [1, 2]]), 1.0),
            inc=("sV", 1))
    tv_last = load_transpose_weight(wv_d, o_wT, False, extra=[t_kproj_T])
    v_last = None
    for tt in range(NT):
        for half in range(2):
            rot = (tt * 2 + half) % 4
            ps = proj_ps[rot]
            for dc in range(NDC):
                def mkV(t, tt=tt, half=half, dc=dc, ps=ps):
                    return t.matmul(ps[:, 0:512], xT[dc][:, tt * 128:(tt + 1) * 128],
                                    wT_tiles[dc][:, half * 512:(half + 1) * 512],
                                    start=(dc == 0), stop=(dc == NDC - 1))
                w = [proj_reader[rot], tv_last] if dc == 0 else []
                em.emit("tensor", mkV, waits=w)
            mmv = em.bump("tensor", "sT")

            def evV(e, tt=tt, half=half, ps=ps):
                src = bass.AP(ps, 0, [[512, 128], [128, 4], [64, 2], [1, 64]])
                dst = bf_ap(o_vaug_t[tt] + half * 4 * PAIR * 2,
                            [[PAIR, 4], [129, 2], [1, 64]])
                return e.tensor_copy(dst, src)
            v_last = em.emit("vector", evV, waits=[mmv, vfill_last], inc=("sV", 1))
            proj_reader[rot] = v_last
    t_vproj_T = em.snap("sT")
    ar.freep("xT")
    ar.freep("wT")

    o_woT = ar.alloc("woT", NDC * 1024 * 2)
    woT = [bf(o_woT + i * 1024 * 2, 1024) for i in range(NDC)]

    o_attnU = ar.alloc("attnU", 8 * 512 * 2)
    attnU = [bf(o_attnU + sl2 * 1024, 512) for sl2 in range(8)]
    o_aoT = ar.alloc("attnoutT", NDC * S * 2)
    aoT = [bf(o_aoT + i * S * 2, S) for i in range(NDC)]
    o_zsb = ar.alloc("zsb", 2 * S * 4)
    Zsb = f32(o_zsb, 2 * S)

    attnU_user = [v_last] * 8
    pAV_user = [v_last, v_last]
    pZR_user = [v_last, v_last]
    pB_user = [v_last, v_last]
    dance_sV_last = [t_cst]
    slot_i = [0]
    pending = []
    wo_done = [None]
    last_av_tok = [None]

    def flush_dance(n):
        for _ in range(min(n, len(pending))):
            pending.pop(0)()

    def emit_dance(h, tok_evacs_done):
        zr = 64 if h % 2 == 0 else 0
        zoff = ((h // 2) % 2) * S
        i, pb = h // 2, (h % 2) * 64
        state = {}

        def stage0():
            for j in range(16):
                def mkz(t, j=j, zr=zr, zoff=zoff):
                    return t.transpose(pZR[:, j:j + 1],
                                       Zsb[zr:zr + 1, zoff + j * 128:zoff + (j + 1) * 128],
                                       consts[zr:zr + 1, 5:6])
                w = [tok_evacs_done, pZR_user[0]] if j == 0 else []
                em.emit("tensor", mkz, waits=w)
            tokz = em.bump("tensor", "sT")
            em.emit("vector", lambda e: e.reciprocal(rsb[:, 0:16], pZR[:, 0:16]), waits=[tokz])
            rc = em.bump("vector", "sV")
            em.emit("tensor", lambda t: t.transpose(pZR[0:16, 16:144], rsb[:, 0:16], ident_f[:, :]),
                    waits=[rc, t_idf])
            tokr = em.bump("tensor", "sT")
            em.emit("vector", lambda e: e.tensor_copy(rrow[:, 0:128], pZR[0:16, 16:144]), waits=[tokr])
            state["rt"] = em.bump("vector", "sV")
            pZR_user[0] = state["rt"]

        def d_qc(qc):
            for j2 in range(4):
                seg = 4 * qc + j2

                def mkb(t, seg=seg, j2=j2, pb=pb):
                    tp = (0, 64) if pb else None
                    return t.matmul(pB[pb:pb + 64, j2 * 128:(j2 + 1) * 128],
                                    selmat[0:16, 64 * seg:64 * seg + 64],
                                    rrow[0:16, 0:128], start=True, stop=True,
                                    tile_position=tp)
                w = [state["rt"], t_const, pB_user[0]] if j2 == 0 else []
                em.emit("tensor", mkb, waits=w)
            tokb = em.bump("tensor", "sT")

            def mknorm(e, qc=qc, i=i, pb=pb):
                sl2 = aoT[i][pb:pb + 64, qc * 512:(qc + 1) * 512]
                return e.tensor_tensor(sl2, sl2, pB[pb:pb + 64, 0:512], ALU.mult)
            em.emit("vector", mknorm, waits=[tokb])
            tokn = em.bump("vector", "sV")
            pB_user[0] = tokn
            dance_sV_last[0] = tokn

        pending.append(stage0)
        for qc in range(4):
            pending.append(lambda qc=qc: d_qc(qc))

    import os as _os2
    _HLIM = int(_os2.environ.get("DBG_ATT_H", H))
    gp_ctr = [0]
    for hp in range(max(1, _HLIM // 2)):
        i = hp
        heads = []
        for par in range(2):
            h = 2 * hp + par
            heads.append(dict(
                h=h, pb=par * 64,
                lhs_q=qh[i][par * 64:par * 64 + 64, :],
                lhs_k=kh[i][par * 64:par * 64 + 64, :],
                av_ps=pAV[par], vbase=hp * PAIR,
                exp_toks=None, pair_slots=None, tok_av=None))
        for qc in range(int(_os2.environ.get("DBG_ATT_QC", 4))):
            if hp == 0 and qc == 2:
                wo_done[0] = load_transpose_weight(wo_d, o_woT, False, extra=[t_vproj_T])
            flush_dance(3)
            K = 4 * qc + 4
            npairs = K // 2
            for hd in heads:
                hd["exp_toks"] = [None] * npairs
                hd["pair_slots"] = [None] * npairs

            def emit_spair2(p, qc=qc):
                gs = (gp_ctr[0] % 2, (gp_ctr[0] + 1) % 2)
                gp_ctr[0] += 2
                diag = 2 * p >= 4 * qc
                toks = [None, None]
                for hi in range(2):
                    hd, g = heads[hi], gs[hi]
                    lhs_k, lhs_q = hd["lhs_k"], hd["lhs_q"]
                    for sub in range(2):
                        kc = 2 * p + sub

                        def mkS(t, kc=kc, g=g, sub=sub, qc=qc, lhs_k=lhs_k,
                                lhs_q=lhs_q, diag=diag):
                            return t.matmul(pS[g][:, sub * 512:(sub + 1) * 512],
                                            lhs_k[:, kc * 128:(kc + 1) * 128],
                                            lhs_q[:, qc * 512:(qc + 1) * 512],
                                            start=True, stop=not diag)
                        w = [pS_ready[g]] if sub == 0 else []
                        if qc == 0 and p == 0 and sub == 0 and hi == 0:
                            w = w + [q_rp[hp // 2], k_rp[hp // 2]]
                        em.emit("tensor", mkS, waits=w)
                        if diag:
                            r = kc - 4 * qc

                            def mkM(t, g=g, sub=sub, r=r):
                                return t.matmul(pS[g][:, sub * 512 + r * 128: sub * 512 + (r + 1) * 128],
                                                ident[:, :], trimask[:, :], start=False, stop=True)
                            em.emit("tensor", mkM, waits=[t_const])
                    toks[hi] = em.bump("tensor", "sT")
                for hi in range(2):
                    hd, g = heads[hi], gs[hi]
                    sl2 = slot_i[0]
                    slot_i[0] = (sl2 + 2) % 8
                    hd["pair_slots"][p] = sl2

                    def mkexp(e, g=g, sl2=sl2):
                        return e.activation(bf(o_attnU + sl2 * 1024, 1024), pS[g][:, 0:1024],
                                            AF.Exp, bias=CST[-20.0], scale=CST[0.125])
                    hd["exp_toks"][p] = em.emit("scalar", mkexp,
                                                waits=[toks[hi], attnU_user[sl2],
                                                       attnU_user[sl2 + 1], t_cst, t_const],
                                                inc=("sS", 1))
                    pS_ready[g] = hd["exp_toks"][p]

            def emit_avpair(hd, p, qc=qc, K=K, extraw=None):
                tokav = None
                av_ps = hd["av_ps"]
                par = hd["pb"] // 64
                vbase = hd["vbase"]
                for sub in range(2):
                    kc = 2 * p + sub
                    sl2 = hd["pair_slots"][p] + sub
                    r = kc - 4 * qc
                    rstart = 128 * r if r > 0 else 0
                    if par == 0:
                        lhsT = vaug[kc][:, vbase:vbase + 65]
                        rows = (0, 65)
                    else:
                        lhsT = vaug[kc][:, vbase + 65:vbase + 193]
                        rows = (0, 128)

                    def mkav(t, kc=kc, sl2=sl2, rstart=rstart, lhsT=lhsT, rows=rows,
                             av_ps=av_ps, K=K):
                        return t.matmul(av_ps[rows[0]:rows[1], rstart:512], lhsT,
                                        attnU[sl2][:, rstart:512],
                                        start=(kc == 0), stop=(kc == K - 1))
                    w = [hd["exp_toks"][p], extraw]
                    if kc == 0:
                        w = w + [pAV_user[par], v_last]
                    em.emit("tensor", mkav, waits=w)
                tokav = em.bump("tensor", "sT")
                attnU_user[hd["pair_slots"][p]] = tokav
                attnU_user[hd["pair_slots"][p] + 1] = tokav
                hd["tok_av"] = tokav
                return tokav

            emit_spair2(0)
            for p in range(1, npairs):
                emit_spair2(p)
                emit_avpair(heads[0], p - 1)
                emit_avpair(heads[1], p - 1)
            emit_avpair(heads[0], npairs - 1)
            emit_avpair(heads[1], npairs - 1)
            last_av_tok[0] = heads[1]["tok_av"]

            for hd in heads:
                par = hd["pb"] // 64
                zr = 64 if par == 0 else 0
                orow = 0 if par == 0 else 64
                av_ps = hd["av_ps"]
                pb = hd["pb"]

                def ev_out(e, i=i, pb=pb, qc=qc, orow=orow, av_ps=av_ps):
                    return e.tensor_copy(aoT[i][pb:pb + 64, qc * 512:(qc + 1) * 512],
                                         av_ps[orow:orow + 64, 0:512])
                em.emit("vector", ev_out, waits=[hd["tok_av"]])

                zoff2 = (hp % 2) * S

                def ev_z(e, zr=zr, qc=qc, av_ps=av_ps, zoff2=zoff2):
                    return e.tensor_copy(Zsb[zr:zr + 1, zoff2 + qc * 512:zoff2 + (qc + 1) * 512],
                                         av_ps[zr:zr + 1, 0:512])
                em.emit("vector", ev_z)
                tok_ev = em.bump("vector", "sV")
                pAV_user[par] = tok_ev
                if qc == 3:
                    emit_dance(hd["h"], tok_ev)
    flush_dance(1000)
    t_attn_T = em.snap("sT")
    t_attn_V = dance_sV_last[0]

    if debug:
        for i2 in range(NDC):
            em.dma("sync", bass.AP(dbg["dbg_qh"].tensor, i2 * 128 * S, [[S, 128], [1, S]]),
                   qh[i2][:, :], waits=[t_attn_T])
            em.dma("sync", bass.AP(dbg["dbg_kh"].tensor, i2 * 128 * S, [[S, 128], [1, S]]),
                   kh[i2][:, :], waits=[t_attn_T])
            em.dma("sync", bass.AP(dbg["dbg_attnoutT"].tensor, i2 * 128 * S, [[S, 128], [1, S]]),
                   aoT[i2][:, :], waits=[t_attn_T, t_attn_V])
        for tt in range(NT):
            em.dma("sync", bass.AP(dbg["dbg_vaug"].tensor, tt * 128 * PAIR * 8,
                                   [[PAIR * 8, 128], [1, PAIR * 8]]),
                   vaug[tt][:, :], waits=[t_attn_T])

    if _HLIM < H:
        em.emit("sync", lambda e: e.wait_ge(sems["dS"], em.cnt["dS"]))
        nc._dbg_off = dict(o_aoT=o_aoT, o_zsb=o_zsb, o_qh=o_qh, o_kh=o_kh,
                           o_vaug_t=o_vaug_t, o_attnU=o_attnU, o_woT=o_woT, ARW=ARW,
                           o_x2hiT=0, o_x2loT=0, o_xst=0, o_xmid=0, o_x2r=0)
        with nc.Block() as block:
            def runner0(eng_name):
                def go(eng):
                    for w, fn, inc in em.ops[eng_name]:
                        for s2, v2 in w:
                            eng.wait_ge(sems[s2], v2)
                        ins = fn(eng)
                        if inc is not None:
                            s2 = inc[0]
                            ins.then_inc(sems[s2], 16 if s2.startswith("d") else 1)
                return go
            block.sync(runner0("sync"))
            block.gpsimd(runner0("gpsimd"))
            block.tensor(runner0("tensor"))
            block.vector(runner0("vector"))
            block.scalar(runner0("scalar"))
        for g2 in reversed(ctx):
            g2.__exit__(None, None, None)
        return nc

    t_dump = ("dS", em.cnt.get("dS", 0)) if debug else None
    ar.freep("qh")
    ar.freep("kh")
    for tt in range(NT):
        ar.freep(f"vaug{tt}")
    ar.freep("attnU")
    ar.freep("zsb")

    o_x2hiT = ar.alloc("x2hiT", NDC * S * 2)
    o_x2loT = ar.alloc("x2loT", NDC * S * 2)
    x2hiT = [bf(o_x2hiT + i2 * S * 2, S) for i2 in range(NDC)]
    x2loT = [bf(o_x2loT + i2 * S * 2, S) for i2 in range(NDC)]
    o_xst = ar.alloc("xst", 3 * 1024 * 4)
    xst = [f32(o_xst + k * 4096, 1024) for k in range(3)]
    o_xmid = ar.alloc("xmid", 3 * 1024 * 4)
    xmid_r = [f32(o_xmid + k * 4096, 1024) for k in range(3)]
    o_x2r = ar.alloc("x2r", 3 * 1024 * 4)
    x2_r = [f32(o_x2r + k * 4096, 1024) for k in range(3)]
    o_xhl = ar.alloc("xhl", 6 * 1024 * 2)
    x2hi_r = [bf(o_xhl + k * 2048, 1024) for k in range(3)]
    x2lo_r = [bf(o_xhl + (3 + k) * 2048, 1024) for k in range(3)]
    o_sq = ar.alloc("sqdump", 3 * 1024 * 4)
    sqd = [f32(o_sq + k * 4096, 1024) for k in range(3)]

    xst_user = [t_dump or t_attn_T, t_dump or t_attn_T, t_dump or t_attn_T]
    xmid_user = [t_dump or t_attn_T] * 3
    x2_user = [[t_dump or t_attn_T], [t_dump or t_attn_T], [t_dump or t_attn_T]]
    xhl_user = [t_dump or t_attn_T] * 3
    sq_user = [t_dump or t_cst] * 3
    ln_guard = [t_cst, t_cst, t_cst]
    spill_toks = [None] * NT
    w_toks = [None] * NT
    hilo_toks = [None] * NT
    last_wo_tok = [None]
    tok_hilo = None
    t_wo_ev = None

    def emit_xld(tt2):
        if tt2 < NT:
            em.dma("scalar", xst[tt2 % 3][:, :],
                   bass.AP(x_d.tensor, tt2 * 128 * D, [[D, 128], [1, D]]),
                   waits=[xst_user[tt2 % 3], t_attn_V], sem="dX")

    def ln1_gen(tt):
        g = tt % 2
        emit_xld(tt + 2)
        for ncol in range(2):
            for dc in range(NDC):
                def mkwo(t, tt=tt, ncol=ncol, dc=dc, g=g):
                    return t.matmul(pS[g][:, ncol * 512:(ncol + 1) * 512],
                                    aoT[dc][:, tt * 128:(tt + 1) * 128],
                                    woT[dc][:, ncol * 512:(ncol + 1) * 512],
                                    start=(dc == 0), stop=(dc == NDC - 1))
                w = []
                if ncol == 0 and dc == 0:
                    w = [pS_ready[g], wo_done[0], t_attn_V]
                em.emit("tensor", mkwo, waits=w)
        tok_wo = em.bump("tensor", "sT")
        if tt == NT - 1:
            last_wo_tok[0] = tok_wo
        xm = xmid_r[tt % 3]
        em.emit("vector", lambda e, xm=xm, g=g, tt=tt: e.tensor_tensor(
            xm, pS[g][:, 0:1024], xst[tt % 3][:, :], ALU.add),
            waits=[tok_wo, ("dX", 16 * min(tt + 3, NT)), xmid_user[tt % 3]])
        tok_xmid = em.bump("vector", "sV")
        pS_ready[g] = tok_xmid
        xst_user[tt % 3] = tok_xmid
        if debug:
            em.dma("sync", bass.AP(dbg["dbg_xmid"].tensor, tt * 128 * D, [[D, 128], [1, D]]),
                   xm, waits=[tok_xmid])
        yield
        ls = lnsc[tt % 3]
        em.emit("vector", lambda e, xm=xm, ls=ls: e.reduce_sum(ls[:, 0:1], xm, axis=AX.X),
                waits=[ln_guard[tt % 3]])
        em.emit("scalar", lambda e, xm=xm, ls=ls, tt=tt: e.activation(
            sqd[tt % 3], xm, AF.Square, bias=CST[0.0], scale=CST[1.0]),
            waits=[tok_xmid, sq_user[tt % 3], t_cst], inc=("sS", 1))
        tok_sq = em.snap("sS")
        em.emit("vector", lambda e, ls=ls, tt=tt: e.reduce_sum(ls[:, 1:2], sqd[tt % 3], axis=AX.X),
                waits=[tok_sq])
        sq_user[tt % 3] = em.bump("vector", "sV")
        em.emit("vector", lambda e, ls=ls: e.tensor_scalar(ls[:, 2:3], ls[:, 0:1], CST[1.0 / D],
                                                           None, ALU.mult))
        em.emit("vector", lambda e, ls=ls: e.tensor_scalar(ls[:, 3:4], ls[:, 1:2], CST[1.0 / D],
                                                           None, ALU.mult))
        em.emit("vector", lambda e, ls=ls: e.tensor_tensor(ls[:, 4:5], ls[:, 2:3], ls[:, 2:3], ALU.mult))
        em.emit("vector", lambda e: e.tensor_copy(consts[:, 8:9], consts[:, 5:6]))
        em.emit("vector", lambda e: e.tensor_copy(consts[:, 9:10], consts[:, 5:6]))
        em.emit("vector", lambda e, ls=ls: e.tensor_tensor(ls[:, 5:6], ls[:, 3:4], ls[:, 4:5], ALU.subtract))
        tok_var = em.bump("vector", "sV")
        em.emit("scalar", lambda e, ls=ls: e.activation(ls[:, 6:7], ls[:, 5:6], AF.Ln,
                                                        bias=CST[1e-5], scale=CST[1.0]),
                waits=[tok_var])
        em.emit("scalar", lambda e, ls=ls: e.activation(ls[:, 7:8], ls[:, 6:7], AF.Exp,
                                                        bias=CST[0.0], scale=CST[-0.5]))
        tok_rstd = em.bump("scalar", "sS")
        yield
        em.emit("vector", lambda e, ls=ls: e.scalar_tensor_tensor(
            ls[:, 8:9], ls[:, 2:3], CST[-1.0], ls[:, 7:8], ALU.mult, ALU.mult), waits=[tok_rstd])
        tok_nmr = em.bump("vector", "sV")
        x2t = x2_r[tt % 3]
        em.emit("scalar", lambda e, xm=xm, x2t=x2t, ls=ls: e.activation(
            x2t, xm, AF.Identity, bias=ls[:, 8:9], scale=ls[:, 7:8]),
            waits=[tok_nmr] + x2_user[tt % 3]
                  + ([("dSP", 16 * (tt - 2))] if tt >= 3 else []))
        tok_x2 = em.bump("scalar", "sS")
        ln_guard[tt % 3] = tok_x2
        xmid_user[tt % 3] = tok_x2
        spill_toks[tt] = em.dma("sync", bass.AP(x2_spill.tensor, tt * 128 * D, [[D, 128], [1, D]]),
                                x2t, waits=[tok_x2], sem="dSP")
        if debug:
            em.dma("sync", bass.AP(dbg["dbg_x2"].tensor, tt * 128 * D, [[D, 128], [1, D]]),
                   x2t, waits=[tok_x2])
        yield
        hi = x2hi_r[tt % 3]
        lo = x2lo_r[tt % 3]
        em.emit("scalar", lambda e, hi=hi, x2t=x2t: e.copy(hi, x2t), waits=[xhl_user[tt % 3]])
        tok_hi = em.bump("scalar", "sS")
        em.emit("vector", lambda e, lo=lo, x2t=x2t, hi=hi: e.tensor_tensor(lo, x2t, hi, ALU.subtract),
                waits=[tok_hi])
        tok_lo = em.bump("vector", "sV")
        x2_user[tt % 3] = [tok_lo]
        for dc in range(NDC):
            em.emit("tensor", lambda t, dc=dc, hi=hi: t.transpose(
                pAV_bf[0][:, dc * 128:(dc + 1) * 128], hi[:, dc * 128:(dc + 1) * 128], ident[:, :]),
                waits=[tok_hi, pAV_user[0]] if dc == 0 else [])
        for dc in range(NDC):
            em.emit("tensor", lambda t, dc=dc, lo=lo: t.transpose(
                pAV_bf[1][:, dc * 128:(dc + 1) * 128], lo[:, dc * 128:(dc + 1) * 128],
                ident[:, :]), waits=[tok_lo, pAV_user[1]] if dc == 0 else [])
        tok_tr = em.bump("tensor", "sT")

        def ev_hi(e, tt=tt):
            dsth = bf_ap(o_x2hiT + tt * 256, [[S, NDC], [1, 128]])
            return e.tensor_copy(dsth, bass.AP(pAV_bf[0], 0, [[1024, 128], [128, NDC], [1, 128]]))
        em.emit("vector", ev_hi, waits=[tok_tr])

        def ev_lo(e, tt=tt):
            dstl = bf_ap(o_x2loT + tt * 256, [[S, NDC], [1, 128]])
            return e.tensor_copy(dstl, bass.AP(pAV_bf[1], 0, [[1024, 128], [128, NDC], [1, 128]]))
        em.emit("vector", ev_lo)
        tok_hilo = em.bump("vector", "sV")
        pAV_user[0] = tok_hilo
        pAV_user[1] = tok_hilo
        xhl_user[tt % 3] = tok_hilo
        hilo_toks[tt] = tok_hilo
        yield

    o_lgt = ar.alloc("lgt", 512 * 4)
    lgt = f32(o_lgt, 512)
    lgt_user = [None]
    tok_pex_prev = [None]

    def emit_router_chunk(ch):
        for ph, (lhsl, rh) in enumerate(((x2hiT, wr_hi), (x2hiT, wr_lo), (x2loT, wr_hi))):
            for dc in range(NDC):
                def mkr(t, ph=ph, dc=dc, lhsl=lhsl, rh=rh, ch=ch):
                    return t.matmul(pZR[0:8, 0:512], rh[:, dc * 8:(dc + 1) * 8],
                                    lhsl[dc][:, ch * 512:(ch + 1) * 512],
                                    start=(ph == 0 and dc == 0), stop=(ph == 2 and dc == NDC - 1))
                w = []
                if ph == 0 and dc == 0:
                    w = [hilo_toks[4 * ch + 3], pZR_user[0], pZR_user[1], t_wrlo]
                em.emit("tensor", mkr, waits=w)
        tok_rt = em.bump("tensor", "sT")
        em.emit("vector", lambda e: e.tensor_copy(lgt[0:8, :], pZR[0:8, 0:512]),
                waits=[tok_rt, lgt_user[0]])
        tok_le = em.bump("vector", "sV")
        pZR_user[0] = tok_le
        for j in range(4):
            em.emit("tensor", lambda t, j=j: t.transpose(pB[:, 8 * j:8 * j + 8],
                                                         lgt[0:8, j * 128:(j + 1) * 128],
                                                         ident_f[0:8, 0:8]),
                    waits=([tok_le, pB_user[0], pB_user[1], t_idf] if j == 0 else []))
        tok_tb = em.bump("tensor", "sT")
        lgt_user[0] = tok_tb
        for j in range(4):
            tt = 4 * ch + j
            base = 8 * j
            lg_ap = pB[:, base:base + 8]
            if debug:
                sl_d = (tt % 2) * 32
                em.emit("vector", lambda e, sl_d=sl_d, lg_ap=lg_ap: e.tensor_copy(
                    lgdump[:, sl_d:sl_d + 8], lg_ap), waits=[tok_tb, lgd_user[tt % 2]])
                tok_lgd = em.bump("vector", "sV")
                lgd_user[tt % 2] = em.dma("sync",
                                          bass.AP(dbg["dbg_lg"].tensor, tt * 128 * NE, [[NE, 128], [1, NE]]),
                                          lgdump[:, sl_d:sl_d + 8], waits=[tok_lgd])
            em.emit("vector", lambda e, lg_ap=lg_ap: e.reduce_max(r8[:, 8:9], lg_ap,
                                                                  axis=AX.X, negate=True),
                    waits=[tok_tb, tok_pex_prev[0]])
            tok_mx = em.bump("vector", "sV")
            em.emit("scalar", lambda e, lg_ap=lg_ap: e.activation(r8[:, 0:8], lg_ap, AF.Exp,
                                                                  bias=r8[:, 8:9], scale=CST[1.0]),
                    waits=[tok_mx])
            tok_pex = em.bump("scalar", "sS")
            tok_pex_prev[0] = tok_pex
            em.emit("vector", lambda e: e.reduce_sum(r8[:, 10:11], r8[:, 0:8], axis=AX.X), waits=[tok_pex])
            em.emit("vector", lambda e: e.reduce_max(r8[:, 12:13], r8[:, 0:8], axis=AX.X))
            em.emit("vector", lambda e: e.reciprocal(r8[:, 11:12], r8[:, 10:11]))
            if debug:
                sl_e = (tt % 2) * 32 + 16
                em.emit("vector", lambda e, sl_e=sl_e: e.tensor_copy(lgdump[:, sl_e:sl_e + 16], r8[:, 0:16]),
                        waits=[lgd_user[2 + tt % 2]])
                tok_r8d = em.bump("vector", "sV")
                lgd_user[2 + tt % 2] = em.dma("sync",
                                              bass.AP(dbg["dbg_r8"].tensor, tt * 128 * 16, [[16, 128], [1, 16]]),
                                              lgdump[:, sl_e:sl_e + 16], waits=[tok_r8d])
            em.emit("vector", lambda e: e.tensor_scalar(r8[:, 16:24], r8[:, 0:8], r8[:, 12:13], None, ALU.is_lt))
            em.emit("vector", lambda e: e.tensor_scalar(r8[:, 24:32], r8[:, 0:8], r8[:, 11:12], None, ALU.mult))
            em.emit("vector", lambda e: e.tensor_tensor(r8[:, 32:40], r8[:, 0:8], r8[:, 16:24], ALU.mult))
            em.emit("vector", lambda e: e.tensor_copy(consts[:, 8:9], consts[:, 5:6]))
            em.emit("vector", lambda e: e.reduce_max(r8[:, 13:14], r8[:, 32:40], axis=AX.X))
            em.emit("vector", lambda e: e.tensor_copy(consts[:, 9:10], consts[:, 5:6]))
            em.emit("vector", lambda e: e.tensor_scalar(r8[:, 40:48], r8[:, 0:8], r8[:, 13:14], None, ALU.is_ge))
            em.emit("vector", lambda e: e.tensor_copy(consts[:, 8:9], consts[:, 5:6]))
            em.emit("vector", lambda e, tt=tt: e.tensor_tensor(w_sb[:, tt * 8:(tt + 1) * 8],
                                                               r8[:, 24:32], r8[:, 40:48], ALU.mult))
            w_toks[tt] = em.bump("vector", "sV")
            if debug:
                em.dma("sync", bass.AP(dbg["dbg_w"].tensor, tt * 128 * NE, [[NE, 128], [1, NE]]),
                       w_sb[:, tt * 8:(tt + 1) * 8], waits=[w_toks[tt]])
        pB_user[0] = tok_pex_prev[0]
        pB_user[1] = tok_pex_prev[0]

    ln1q = []

    def ln1_step():
        for g2 in ln1q[:]:
            try:
                next(g2)
            except StopIteration:
                ln1q.remove(g2)

    emit_xld(0)
    emit_xld(1)
    rch = 0
    for tt in range(NT + 4):
        ln1_step()
        if tt < NT:
            gg = ln1_gen(tt)
            next(gg)
            ln1q.append(gg)
        while rch < 4 and hilo_toks[4 * rch + 3] is not None:
            emit_router_chunk(rch)
            rch += 1
    assert not ln1q and rch == 4

    if debug:
        for i2 in range(NDC):
            em.dma("sync", bass.AP(dbg["dbg_x2loT"].tensor, i2 * 128 * S, [[S, 128], [1, S]]),
                   x2loT[i2][:, :], waits=[w_toks[-1]])
    t_ln_T = em.snap("sT")
    t_ln_S = em.snap("sS")
    ar.freep("attnoutT")
    ar.freep("woT")
    ar.freep("x2loT")
    ar.freep("xst")
    ar.freep("xmid")
    ar.freep("x2r")
    ar.freep("xhl")

    o_h = ar.alloc("hsb", 2 * NDC * 1024 * 2)
    h_sb = [[bf(o_h + (b * NDC + jc) * 2048, 1024) for jc in range(NDC)] for b in range(2)]
    o_wm = ar.alloc("wm", 2 * NDC * 1024 * 2)
    wm_sb = [[bf(o_wm + (b * NDC + jc) * 2048, 1024) for jc in range(NDC)] for b in range(2)]
    o_wi = ar.alloc("wi", 2 * NDC * 1024 * 2)
    wi_sb = [[bf(o_wi + (b * NDC + dc) * 2048, 1024) for dc in range(NDC)] for b in range(2)]
    o_oacc_t = [ar.alloc(f"oacc{t2}", 1024 * 4) for t2 in range(8)]
    oacc = [f32(o_oacc_t[t2], 1024) for t2 in range(8)]
    o_x2b = ar.alloc("x2back", 2 * 1024 * 4)
    x2b = [f32(o_x2b + k * 4096, 1024) for k in range(2)]
    o_outr = ar.alloc("outr", 2 * 1024 * 4)
    outr = [f32(o_outr + k * 4096, 1024) for k in range(2)]

    pool_guard = [t_ln_T, t_ln_S, w_toks[-1], spill_toks[-1]]
    if debug:
        pool_guard.append(("dS", em.cnt.get("dS", 0)))
    wi_guard = [last_wo_tok[0]]
    if debug:
        wi_guard = wi_guard + [("dS", em.cnt.get("dS", 0))]
    wi_user = [wi_guard, wi_guard]
    wm_user = [pool_guard, pool_guard]
    h_user = [pool_guard, pool_guard]
    s1ev_user = [pAV_user[0], pAV_user[1]]
    oacc_user = [pool_guard] * 8
    x2b_user = [pool_guard, pool_guard]
    outr_user = [pool_guard, pool_guard]
    store_toks = []

    def _wl(u):
        return u if isinstance(u, list) else [u]

    def load_expert(e_i, buf):
        t1 = em.dma("gpsimd", bf_ap(o_wi + buf * NDC * 2048, [[1024, NDC], [1, 1024]]),
                    bass.AP(wi_d.tensor, e_i * D * D, [[D, 128], [128 * D, NDC], [1, D]]),
                    waits=_wl(wi_user[buf]))
        t2_ = em.dma("gpsimd", bf_ap(o_wm + buf * NDC * 2048, [[1024, NDC], [1, 1024]]),
                     bass.AP(wm_d.tensor, e_i * D * D, [[D, 128], [128 * D, NDC], [1, D]]),
                     waits=_wl(wm_user[buf]))
        return (t1, t2_)

    ld_toks = {(0, 0): load_expert(0, 0)}
    for half in range(2):
        def ln2_gen(t2, half=half):
            tt_g = half * 8 + t2
            xm2 = oacc[t2]
            wadd = _wl(oacc_user[t2])
            if debug:
                td = em.dma("sync", bass.AP(dbg["dbg_oacc"].tensor, tt_g * 128 * D,
                                            [[D, 128], [1, D]]),
                            oacc[t2], waits=_wl(oacc_user[t2]))
                wadd = wadd + [td]
            t_x2ld = em.dma("sync", x2b[t2 % 2][:, :],
                            bass.AP(x2_spill.tensor, tt_g * 128 * D, [[D, 128], [1, D]]),
                            waits=_wl(x2b_user[t2 % 2]) + [("dSP", 16 * NT)], sem="dB")
            em.emit("vector", lambda e2, t2=t2, xm2=xm2: e2.tensor_tensor(
                xm2, x2b[t2 % 2], xm2, ALU.add), waits=[t_x2ld] + wadd)
            tok_xm2 = em.bump("vector", "sV")
            x2b_user[t2 % 2] = tok_xm2
            oacc_user[t2] = tok_xm2
            ls = lnsc[t2 % 3]
            em.emit("vector", lambda e2, xm2=xm2, ls=ls: e2.reduce_sum(ls[:, 0:1], xm2, axis=AX.X),
                    waits=[ln_guard[t2 % 3]])
            tok_c = em.bump("vector", "sV")
            yield
            em.emit("scalar", lambda e2, xm2=xm2, t2=t2: e2.activation(
                sqd[t2 % 3], xm2, AF.Square, bias=CST[0.0], scale=CST[1.0]),
                waits=[tok_c, sq_user[t2 % 3]], inc=("sS", 1))
            tok_sq = em.snap("sS")
            em.emit("vector", lambda e2, ls=ls, t2=t2: e2.reduce_sum(ls[:, 1:2], sqd[t2 % 3], axis=AX.X),
                    waits=[tok_sq])
            sq_user[t2 % 3] = em.bump("vector", "sV")
            em.emit("vector", lambda e2, ls=ls: e2.tensor_scalar(ls[:, 2:3], ls[:, 0:1], CST[1.0 / D],
                                                                 None, ALU.mult))
            em.emit("vector", lambda e2, ls=ls: e2.tensor_scalar(ls[:, 3:4], ls[:, 1:2], CST[1.0 / D],
                                                                 None, ALU.mult))
            em.emit("vector", lambda e2, ls=ls: e2.tensor_tensor(ls[:, 4:5], ls[:, 2:3], ls[:, 2:3], ALU.mult))
            em.emit("vector", lambda e2: e2.tensor_copy(consts[:, 8:9], consts[:, 5:6]))
            em.emit("vector", lambda e2: e2.tensor_copy(consts[:, 9:10], consts[:, 5:6]))
            em.emit("vector", lambda e2, ls=ls: e2.tensor_tensor(ls[:, 5:6], ls[:, 3:4], ls[:, 4:5], ALU.subtract))
            tok_var = em.bump("vector", "sV")
            em.emit("scalar", lambda e2, ls=ls: e2.activation(ls[:, 6:7], ls[:, 5:6], AF.Ln,
                                                              bias=CST[1e-5], scale=CST[1.0]),
                    waits=[tok_var])
            em.emit("scalar", lambda e2, ls=ls: e2.activation(ls[:, 7:8], ls[:, 6:7], AF.Exp,
                                                              bias=CST[0.0], scale=CST[-0.5]))
            tok_rstd = em.bump("scalar", "sS")
            yield
            em.emit("vector", lambda e2, ls=ls: e2.scalar_tensor_tensor(
                ls[:, 8:9], ls[:, 2:3], CST[-1.0], ls[:, 7:8], ALU.mult, ALU.mult), waits=[tok_rstd])
            tok_nmr = em.bump("vector", "sV")
            ot = outr[t2 % 2]
            em.emit("scalar", lambda e2, xm2=xm2, ot=ot, ls=ls: e2.activation(
                ot, xm2, AF.Identity, bias=ls[:, 8:9], scale=ls[:, 7:8]),
                waits=[tok_nmr] + _wl(outr_user[t2 % 2]))
            tok_out = em.bump("scalar", "sS")
            ln_guard[t2 % 3] = tok_out
            oacc_user[t2] = tok_out
            st = em.dma("sync", bass.AP(out_d.tensor, tt_g * 128 * D, [[D, 128], [1, D]]),
                        outr[t2 % 2], waits=[tok_out], sem="dO")
            outr_user[t2 % 2] = st
            store_toks.append(st)
            yield

        if half == 0:
            ln2q = []

        def ln2_step():
            for g in ln2q[:]:
                try:
                    next(g)
                except StopIteration:
                    ln2q.remove(g)

        def ln2_push(t2):
            g = ln2_gen(t2)
            next(g)
            ln2q.append(g)
        for e_i in range(NE):
            buf = e_i % 2
            nxt = (half, e_i + 1) if e_i + 1 < NE else ((1, 0) if half == 0 else None)
            if nxt is not None and nxt not in ld_toks:
                ld_toks[nxt] = load_expert(nxt[1], nxt[1] % 2)
            tw = ld_toks[(half, e_i)]
            ev_last = None
            for jc in range(NDC):
                for tc in range(2):
                    rot = (jc * 2 + tc) % 2
                    ps = pAV[rot]
                    for dc in range(NDC):
                        def mk1(t, jc=jc, tc=tc, dc=dc, ps=ps, buf=buf, half=half):
                            return t.matmul(ps[:, 0:512],
                                            wi_sb[buf][dc][:, jc * 128:(jc + 1) * 128],
                                            x2hiT[dc][:, half * 1024 + tc * 512: half * 1024 + (tc + 1) * 512],
                                            start=(dc == 0), stop=(dc == NDC - 1))
                        w = []
                        if dc == 0:
                            w = _wl(s1ev_user[rot]) + [tw[0]]
                            if jc == 0 and tc == 0:
                                w = w + [tok_hilo]
                        em.emit("tensor", mk1, waits=w)
                    s1_mm_last = em.bump("tensor", "sT")
                    if jc == NDC - 1 and tc == 1:
                        wi_user[buf] = s1_mm_last

                    def ev1(e2, jc=jc, tc=tc, ps=ps, buf=buf):
                        return e2.copy(h_sb[buf][jc][:, tc * 512:(tc + 1) * 512], ps[:, 0:512])
                    em.emit("scalar", ev1, waits=[s1_mm_last] + _wl(h_user[buf]))
                    ev_last = em.bump("scalar", "sS")
                    s1ev_user[rot] = ev_last
                    if e_i == 0 and ln2q:
                        ln2_step()
            while ln2q:
                ln2_step()
            for t2 in range(8):
                tt_g = half * 8 + t2
                g = t2 % 2
                for ncol in range(2):
                    for jc in range(NDC):
                        def mk2(t, t2=t2, ncol=ncol, jc=jc, g=g, buf=buf):
                            return t.matmul(pS[g][:, ncol * 512:(ncol + 1) * 512],
                                            h_sb[buf][jc][:, t2 * 128:(t2 + 1) * 128],
                                            wm_sb[buf][jc][:, ncol * 512:(ncol + 1) * 512],
                                            start=(jc == 0), stop=(jc == NDC - 1))
                        w = []
                        if ncol == 0 and jc == 0:
                            w = _wl(pS_ready[g]) + [ev_last, tw[1]]
                        em.emit("tensor", mk2, waits=w)
                s2_mm_last = em.bump("tensor", "sT")
                if t2 == 7:
                    wm_user[buf] = s2_mm_last
                    h_user[buf] = s2_mm_last
                wcol = w_sb[:, tt_g * 8 + e_i: tt_g * 8 + e_i + 1]
                if e_i == 0:
                    em.emit("vector", lambda e2, t2=t2, g=g, wcol=wcol: e2.tensor_scalar(
                        oacc[t2], pS[g][:, 0:1024], wcol, None, ALU.mult),
                        waits=[s2_mm_last, w_toks[tt_g]] + _wl(oacc_user[t2]))
                else:
                    em.emit("vector", lambda e2, t2=t2, g=g, wcol=wcol: e2.scalar_tensor_tensor(
                        oacc[t2], pS[g][:, 0:1024], wcol, oacc[t2], ALU.mult, ALU.add),
                        waits=[s2_mm_last])
                tok_o = em.bump("vector", "sV")
                pS_ready[g] = tok_o
                oacc_user[t2] = tok_o
                if e_i == NE - 1:
                    ln2_step()
                    ln2_push(t2)

        if half == 1:
            while ln2q:
                ln2_step()

    if debug:
        for i2 in range(NDC):
            em.dma("sync", bass.AP(dbg["dbg_x2hiT"].tensor, i2 * 128 * S, [[S, 128], [1, S]]),
                   x2hiT[i2][:, :], waits=[store_toks[-1]])
    em.emit("sync", lambda e: e.wait_ge(sems["dO"], em.cnt["dO"]))
    em.emit("sync", lambda e: e.wait_ge(sems["dSP"], em.cnt["dSP"]))
    if em.cnt.get("dS"):
        em.emit("sync", lambda e: e.wait_ge(sems["dS"], em.cnt["dS"]))
    em.emit("gpsimd", lambda e: e.wait_ge(sems["dG"], em.cnt["dG"]))

    with nc.Block() as block:
        def runner(eng_name):
            def go(eng):
                for w, fn, inc in em.ops[eng_name]:
                    for s2, v2 in w:
                        eng.wait_ge(sems[s2], v2)
                    ins = fn(eng)
                    if inc is not None:
                        s2 = inc[0]
                        ins.then_inc(sems[s2], 16 if s2.startswith("d") else 1)
            return go
        block.sync(runner("sync"))
        block.gpsimd(runner("gpsimd"))
        block.tensor(runner("tensor"))
        block.vector(runner("vector"))
        block.scalar(runner("scalar"))

    nc._dbg_off = dict(o_aoT=o_aoT, o_zsb=o_zsb, o_x2hiT=o_x2hiT, o_x2loT=o_x2loT,
                       o_xst=o_xst, o_xmid=o_xmid, o_x2r=o_x2r, o_qh=o_qh, o_kh=o_kh,
                       o_vaug_t=o_vaug_t, o_attnU=o_attnU, o_woT=o_woT, ARW=ARW)
    for g2 in reversed(ctx):
        g2.__exit__(None, None, None)
    return nc


_HOST_CONSTS = None


def _host_consts():
    global _HOST_CONSTS
    if _HOST_CONSTS is None:
        import ml_dtypes
        pos = np.arange(S, dtype=np.float64)
        inv = 10000.0 ** (-(2.0 * np.arange(32, dtype=np.float64)) / 64.0)
        ang = np.outer(inv, pos)
        cosT = np.tile(np.cos(ang), (4, 1)).astype(np.float32)
        sinT = np.tile(np.sin(ang), (4, 1)).astype(np.float32)
        identb = np.eye(128, dtype=np.float32).astype(ml_dtypes.bfloat16)
        k_idx = np.arange(128)[:, None]
        q_idx = np.arange(128)[None, :]
        trimask = np.where(k_idx <= q_idx, np.float32(0.0),
                           np.float32(-1e9)).astype(ml_dtypes.bfloat16)
        selm = np.zeros((16, 1024), np.float32)
        for s_i in range(16):
            selm[s_i, 64 * s_i:64 * s_i + 64] = 1.0
        selm = selm.astype(ml_dtypes.bfloat16)
        _HOST_CONSTS = (np.ascontiguousarray(cosT), np.ascontiguousarray(sinT),
                        np.ascontiguousarray(identb), np.ascontiguousarray(trimask),
                        np.ascontiguousarray(selm))
    return _HOST_CONSTS


def kernel(L0_attn_x, L0_attn_wq, L0_attn_wk, L0_attn_wv, L0_attn_wo,
           L0_ffn_Wr, L0_ffn_Wi, L0_ffn_Wo):
    cosT, sinT, identb, trimask, selm = _host_consts()
    nc = build_nc(debug=False)
    shared = {
        "wq": np.ascontiguousarray(np.asarray(L0_attn_wq, dtype=np.float32)),
        "wk": np.ascontiguousarray(np.asarray(L0_attn_wk, dtype=np.float32)),
        "wv": np.ascontiguousarray(np.asarray(L0_attn_wv, dtype=np.float32)),
        "wo": np.ascontiguousarray(np.asarray(L0_attn_wo, dtype=np.float32)),
        "wr": np.ascontiguousarray(np.asarray(L0_ffn_Wr, dtype=np.float32)),
        "wi": np.ascontiguousarray(np.asarray(L0_ffn_Wi, dtype=np.float32)),
        "wm": np.ascontiguousarray(np.asarray(L0_ffn_Wo, dtype=np.float32)),
        "costab": cosT, "sintab": sinT, "identb": identb, "trimask": trimask,
        "selmat": selm,
    }
    xs = np.asarray(L0_attn_x, dtype=np.float32)
    in_maps = []
    for c in range(8):
        m = dict(shared)
        m["x"] = np.ascontiguousarray(xs[c])
        in_maps.append(m)
    res = run_bass_kernel_spmd(nc, in_maps, core_ids=list(range(8)))
    out = np.stack([np.asarray(res.results[c]["out"]) for c in range(8)], axis=0)
    return out.astype(np.float32)
